# version 11
# speedup vs baseline: 1.3257x; 1.0240x over previous
import math
from contextlib import ExitStack

import numpy as np

import concourse.bass as bass
import concourse.mybir as mybir
from concourse.bass_utils import run_bass_kernel_spmd

F32 = mybir.dt.float32
BF16 = mybir.dt.bfloat16
AF = mybir.ActivationFunctionType
ALU = mybir.AluOpType
AX = mybir.AxisListType

ENGINES = ("pe", "act", "dve", "pool", "sp")
_ENG_ATTR = {"pe": "tensor", "act": "scalar", "dve": "vector", "pool": "gpsimd", "sp": "sync"}


class K:

    def __init__(self, name="mk"):
        self.nc = bass.Bass("TRN2", target_bir_lowering=False)
        self._stack = ExitStack()
        self.ops = {e: [] for e in ENGINES}
        self.count = {e: 0 for e in ENGINES}
        self.dma_groups = {}
        self._sem_specs = []
        self._eng_sem_i = {}
        self._eng_sems_all = {e: [] for e in ENGINES}
        self._eng_closed = []
        for e in ENGINES:
            self._eng_sem_i[e] = self._new_sem(f"sem_{e}")
            self._eng_sems_all[e].append(self._eng_sem_i[e])
        self.track = {}
        self.water = {e: {} for e in ENGINES}
        self._sbufs = {}
        self._psums = {}
        self._drams = {}
        self._finalized = False

        self._arena_waits = []
        self._dma_sem_total = {}
        self._dma_consumers = {}
        self._dma_slot_caps = {"hw": 80, "sw": 18}
        self._dma_slots = {"hw": [], "sw": []}
        self._dma_rr = {"hw": 0, "sw": 0}
        self._sem_counts = {}
        self._sem_last_use = {}
        self._dma_seq = 0
        self._alloc_seq = 0
        self._sbuf_guards = {}
        self.consts = {}
        self._sbuf_bytes = 0

    def _new_sem(self, name):
        self._sem_specs.append(name)
        return len(self._sem_specs) - 1

    def dram(self, name, shape, dtype, kind="Internal", addr_space="Local"):
        t = self.nc.dram_tensor(name, list(shape), dtype, kind=kind, addr_space=addr_space)
        self._drams[name] = t
        return t

    def sbuf(self, name, shape, dtype):
        guard = self.nc.sbuf_tensor(name, list(shape), dtype)
        h = guard.__enter__()
        self._alloc_seq += 1
        self._sbuf_guards[name] = (self._alloc_seq, guard)
        self._sbufs[name] = h
        n = 1
        for s in shape[1:]:
            n *= s
        self._sbuf_bytes += n * (2 if dtype == BF16 else 4) * 128
        return h

    def free(self, *names):
        entries = sorted((self._sbuf_guards.pop(n) for n in names), key=lambda e: -e[0])
        for _, g in entries:
            g.__exit__(None, None, None)
        for n in names:
            self._sbufs.pop(n, None)
            self._psums.pop(n, None)
        self.barrier()

    def barrier(self):
        toks = [(self._eng_sem_i[e], self.count[e]) for e in ENGINES if self.count[e]]
        toks += list(self._eng_closed)
        toks += [(s, v) for s, v in self._dma_sem_total.items() if v]
        self._arena_waits = toks

    def const(self, name, arr, dtype=None):
        import ml_dtypes
        a0 = arr[0] if isinstance(arr, list) else arr
        if dtype is None:
            dtype = BF16 if a0.dtype == ml_dtypes.bfloat16 else F32
        t = self.dram(name, a0.shape, dtype, kind="ExternalInput")
        self.consts[name] = arr
        return t

    def psum(self, name, shape, dtype=F32):
        guard = self.nc.psum_tensor(name, list(shape), dtype)
        h = guard.__enter__()
        self._alloc_seq += 1
        self._sbuf_guards[name] = (self._alloc_seq, guard)
        self._psums[name] = h
        return h

    def _need(self, key):
        return self.track.setdefault(key, {"w": None, "r": []})

    def _gather_waits(self, eng, reads, writes, extra_waits):
        waits = {}
        self._waited_dma_sems = set()

        def add(tok):
            if tok is None:
                return
            sem_i, val = tok
            tot = self._dma_sem_total.get(sem_i)
            if tot is not None:
                val = tot
                self._waited_dma_sems.add(sem_i)
            if eng == "pe" and sem_i in self._eng_sems_all["pe"]:
                return
            if waits.get(sem_i, 0) < val:
                waits[sem_i] = val

        for k in reads:
            add(self._need(k)["w"])
        for k in writes:
            e = self._need(k)
            add(e["w"])
            for t in e["r"]:
                add(t)
        for t in extra_waits or ():
            add(t)
        for t in self._arena_waits:
            add(t)
        w = self.water[eng]
        out = []
        for sem_i, val in sorted(waits.items()):
            if w.get(sem_i, 0) >= val:
                continue
            w[sem_i] = val
            out.append((sem_i, val))
        return out

    def _commit(self, tok, reads, writes):
        for k in reads:
            self._need(k)["r"].append(tok)
        for k in writes:
            e = self._need(k)
            e["w"] = tok
            e["r"] = []

    def _note_dma_consumer(self, tok):
        for sem_i in self._waited_dma_sems:
            self._dma_consumers.setdefault(sem_i, []).append(tok)
        self._waited_dma_sems = set()

    def op(self, eng, fn, reads=(), writes=(), extra_waits=()):
        assert not self._finalized
        if self.count[eng] >= 30000:
            self._eng_closed.append((self._eng_sem_i[eng], self.count[eng]))
            self._eng_sem_i[eng] = self._new_sem(f"sem_{eng}{len(self._eng_sems_all[eng])}")
            self._eng_sems_all[eng].append(self._eng_sem_i[eng])
            self.count[eng] = 0
        waits = self._gather_waits(eng, reads, writes, extra_waits)
        self.count[eng] += 1
        tok = (self._eng_sem_i[eng], self.count[eng])
        self.ops[eng].append((waits, fn, (self._eng_sem_i[eng], 1)))
        self._commit(tok, reads, writes)
        self._note_dma_consumer(tok)
        return tok

    def dma(self, eng, out_ap_fn, in_ap_fn, group, reads=(), writes=(), **kw):
        assert not self._finalized
        cls = "sw" if eng == "pool" else "hw"
        if "_cast" in group:
            group = "cast"
        else:
            for pref in ("l0f0", "l0f1", "l0a", "l0f", "l1a", "l1f",
                         "l2a", "l2f", "l3a", "l3f"):
                if group.startswith(pref):
                    group = group[len(pref):]
                    break
            if group in ("c2", "c3", "bl", "br", "sv", "cnt", "init", "consts", "prolog"):
                group = "c"
            elif group in ("Wkout", "Wqout", "wWk", "wWq", "wWv"):
                group = "wW"
        group = cls + ":" + group
        g = self.dma_groups.get(group)
        if g is None:
            slots = self._dma_slots[cls]
            if len(slots) < self._dma_slot_caps[cls]:
                sem_i = self._new_sem(f"dma_{cls}{len(slots)}")
                slots.append(sem_i)
                self._sem_counts[sem_i] = 0
            else:
                sem_i = min(slots, key=lambda s: self._sem_last_use.get(s, -1))
            g = self.dma_groups[group] = {"sem_i": sem_i}
        self._dma_seq += 1
        self._sem_last_use[g["sem_i"]] = self._dma_seq
        prev_consumers = tuple(self._dma_consumers.get(g["sem_i"], ()))
        waits = self._gather_waits(eng, reads, writes, prev_consumers)
        cnt = self._sem_counts[g["sem_i"]] + 16
        self._sem_counts[g["sem_i"]] = cnt
        self._dma_sem_total[g["sem_i"]] = cnt
        tok = (g["sem_i"], cnt)

        def fn(e, _o=out_ap_fn, _i=in_ap_fn, _kw=kw, _sem_i=g["sem_i"]):
            o = _o() if callable(_o) else _o
            i = _i() if callable(_i) else _i
            return e.dma_start(out=o, in_=i, **_kw)

        self.ops[eng].append((waits, fn, (g["sem_i"], 16)))
        self._commit(tok, reads, writes)
        self._note_dma_consumer(tok)
        return tok

    def wait(self, eng, tok):
        waits = self._gather_waits(eng, (), (), (tok,))
        self._waited_dma_sems = set()
        if waits:
            self.ops[eng].append((waits, None, None))

    def barrier_token(self, eng):
        return (self._eng_sem_i[eng], self.count[eng])

    def collective(self, kind, op, ins, outs, name, reads=(), writes=(), extra_waits=()):
        sem_i = self._new_sem(f"cc_{name}")
        waits = self._gather_waits("pool", reads, writes, extra_waits)

        def fn(e, _k=kind, _op=op, _ins=ins, _outs=outs):
            return e.collective_compute(
                _k, _op, replica_groups=[list(range(8))],
                ins=[a() if callable(a) else a for a in _ins],
                outs=[a() if callable(a) else a for a in _outs])

        self.ops["pool"].append((waits, fn, (sem_i, 1)))
        tok = (sem_i, 1)
        self._commit(tok, reads, writes)
        self._note_dma_consumer(tok)
        return tok

    def dep(self, key):
        return self._need(key)["w"]

    def mm(self, out, lhsT, rhs, start, stop, reads=(), writes=()):
        return self.op("pe", lambda e, o=out, l=lhsT, r=rhs, s1=start, s2=stop:
                       e.matmul(o, l, r, start=s1, stop=s2), reads=reads, writes=writes)

    def transp(self, out, in_, ident, reads=(), writes=()):
        return self.op("pe", lambda e, o=out, i=in_, d=ident: e.transpose(o, i, d),
                       reads=reads, writes=writes)

    def tt(self, eng, out, a, b, alu, reads=(), writes=()):
        return self.op(eng, lambda e, o=out, x=a, y=b, u=alu: e.tensor_tensor(o, x, y, u),
                       reads=reads, writes=writes)

    def ts(self, eng, out, in0, s1, s2, op0, op1=None, reads=(), writes=()):
        if op1 is None:
            return self.op(eng, lambda e, o=out, i=in0, a=s1, u=op0: e.tensor_scalar(o, i, a, None, u),
                           reads=reads, writes=writes)
        return self.op(eng, lambda e, o=out, i=in0, a=s1, b=s2, u=op0, v=op1:
                       e.tensor_scalar(o, i, a, b, u, v), reads=reads, writes=writes)

    def stt(self, eng, out, in0, scalar, in1, op0, op1, reads=(), writes=()):
        return self.op(eng, lambda e, o=out, i=in0, s=scalar, j=in1, u=op0, v=op1:
                       e.scalar_tensor_tensor(o, i, s, j, u, v), reads=reads, writes=writes)

    def act(self, out, in_, func, bias=0.0, scale=1.0, accum_out=None, reads=(), writes=()):
        return self.op("act", lambda e, o=out, i=in_, f=func, b=bias, s=scale, a=accum_out:
                       e.activation(o, i, f, bias=b, scale=s, accum_out=a),
                       reads=reads, writes=writes)

    def copy(self, eng, out, in_, reads=(), writes=()):
        return self.op(eng, lambda e, o=out, i=in_: e.tensor_copy(o, i), reads=reads, writes=writes)

    def red(self, out, in_, axis, alu, reads=(), writes=(), negate=None):
        def f(e, o=out, i=in_, a=axis, u=alu, n=negate):
            if u == ALU.max:
                return e.reduce_max(o, i, a, negate=n)
            if u == ALU.add:
                return e.reduce_sum(o, i, a, negate=n)
            return e.tensor_reduce(o, i, a, u, negate=n)
        return self.op("dve", f, reads=reads, writes=writes)

    def memset(self, eng, ap, val, reads=(), writes=()):
        return self.op(eng, lambda e, a=ap, v=val: e.memset(a, v), reads=reads, writes=writes)

    def recip(self, out, in_, reads=(), writes=()):
        return self.op("dve", lambda e, o=out, i=in_: e.reciprocal(o, i), reads=reads, writes=writes)

    def finalize(self):
        assert not self._finalized
        drain = [(s, v) for s, v in self._dma_sem_total.items() if v]
        drain += [(self._eng_sem_i[e], self.count[e]) for e in ENGINES if self.count[e]]
        drain += list(self._eng_closed)
        waits = self._gather_waits("sp", (), (), drain)
        if waits:
            self.ops["sp"].append((waits, None, None))
        self._finalized = True
        nc = self.nc
        sems = [self._stack.enter_context(nc.semaphore(n)) for n in self._sem_specs]
        with nc.Block() as block:
            for e in ENGINES:
                if not self.ops[e]:
                    continue

                def mk_body(ename):
                    def body(eng):
                        for waits, fn, inc in self.ops[ename]:
                            for sem_i, val in waits:
                                eng.wait_ge(sems[sem_i], val)
                            if fn is not None:
                                inst = fn(eng)
                                if inc is not None:
                                    inst.then_inc(sems[inc[0]], inc[1])
                    return body

                getattr(block, _ENG_ATTR[e])(mk_body(e))
        return nc

    def in_maps(self, per_core):
        maps = []
        for c, base in enumerate(per_core):
            m = dict(base)
            for name, arr in self.consts.items():
                m[name] = arr[c] if isinstance(arr, list) else arr
            maps.append(m)
        return maps

    def close(self):
        for _, g in sorted(self._sbuf_guards.values(), key=lambda e: -e[0]):
            g.__exit__(None, None, None)
        self._sbuf_guards.clear()
        self._stack.close()



def _ceil_div(a, b):
    return -(-a // b)


S, D, NCH = 4096, 1024, 8


def _bf16(a):
    import ml_dtypes
    return np.asarray(a).astype(ml_dtypes.bfloat16)


def bcast_free(ap, n):
    return bass.AP(ap.tensor, ap.offset, [list(x) for x in ap.ap] + [[0, n]])


def bcast_at(ap, n, at):
    dims = [list(x) for x in ap.ap]
    return bass.AP(ap.tensor, ap.offset, dims[:at] + [[0, n]] + dims[at:])



def emit_common_consts(k):
    k.const("c_ident_f", np.eye(128, dtype=np.float32))
    k.const("c_ident_b", _bf16(np.eye(128)))
    k.const("c_ones_col_b", _bf16(np.ones((128, 1))))
    k.const("c_ones_row_f", np.ones((1, 128), np.float32))
    k.const("c_ones_row_b", _bf16(np.ones((1, 128))))
    k.const("c_ones_col_f", np.ones((128, 1), np.float32))
    ident_f = k.sbuf("ident_f", (128, 128), F32)
    ident_b = k.sbuf("ident_b", (128, 128), BF16)
    ones_col_b = k.sbuf("ones_col_b", (128, 1), BF16)
    ones_row_f = k.sbuf("ones_row_f", (1, 128), F32)
    ones_row_b = k.sbuf("ones_row_b", (1, 128), BF16)
    ones_col_f = k.sbuf("ones_col_f", (128, 1), F32)
    zrow_b = k.sbuf("zrow_b", (1, 512), BF16)
    k.memset("dve", zrow_b[:, :], 0.0, writes=[("zrow_b",)])
    for n in ("ident_f", "ident_b", "ones_col_b", "ones_row_f", "ones_row_b", "ones_col_f"):
        k.dma("sp", k._sbufs[n][:, :], k._drams["c_" + n].ap(), "consts", writes=[(n,)])


def load_fm(k, name, src_dram, group, n_ch=NCH, cols=S, dtype=BF16, reads=()):
    t = k.sbuf(name, (128, n_ch, cols), dtype)
    for c in range(n_ch):
        k.dma("sp", t[:, c, :], src_dram.ap()[:, 128 * c:128 * (c + 1)], group,
              reads=reads, writes=[(name, c)], transpose=True)
    return t


def load_wT(k, name, w_dram, group, dout=D, din=D):
    scr = k.dram(name + "_scr", (dout, din), BF16)
    k.dma("pool", scr.ap(), w_dram.ap(), group + "_cast", writes=[(name, "scr")])
    t = k.sbuf(name, (128, din // 128, dout), BF16)
    for c in range(din // 128):
        k.dma("sp", t[:, c, :], scr.ap()[:, 128 * c:128 * (c + 1)], group,
              reads=[(name, "scr")], writes=[(name, c)], transpose=True)
    return t


def psum_zero(k, ps_full_ap, pkeys):
    k.mm(ps_full_ap, k._sbufs["ones_row_b"][:, :], k._sbufs["zrow_b"][:, :], True, False,
         reads=[("ones_row_b",), ("zrow_b",)], writes=list(pkeys))


def psum_close(k, ps_full_ap, pkeys):
    k.mm(ps_full_ap, k._sbufs["ones_row_b"][:, :], k._sbufs["zrow_b"][:, :], False, True,
         reads=[("ones_row_b",), ("zrow_b",)], writes=list(pkeys))


class LNKit:

    def __init__(self, k, pref):
        self.k = k
        self.pref = pref
        self.m = k.sbuf(pref + "_m", (128, 1), F32)
        self.ms = k.sbuf(pref + "_ms", (128, 1), F32)
        self.r = k.sbuf(pref + "_r", (128, 1), F32)
        self.sq = k.sbuf(pref + "_sq", (128, D), F32)
        self.eps = k.sbuf(pref + "_eps", (128, 1), F32)
        self._eps_val = None

    def free(self):
        self.k.free(self.pref + "_m", self.pref + "_ms", self.pref + "_r", self.pref + "_sq",
                    self.pref + "_eps")

    def apply(self, z_ap, out_ap, zkey, outkey, eps=1e-5, cols=D):
        k, p = self.k, self.pref
        inv = 1.0 / cols
        k.red(self.m[:, 0:1], z_ap, AX.X, ALU.add, reads=[zkey], writes=[(p, "m")])
        k.act(self.sq[:, :cols], z_ap, AF.Square, reads=[zkey], writes=[(p, "sq")])
        k.red(self.ms[:, 0:1], self.sq[:, :cols], AX.X, ALU.add, reads=[(p, "sq")], writes=[(p, "ms")])
        k.ts("dve", self.m[:, 0:1], self.m[:, 0:1], inv, None, ALU.mult,
             reads=[(p, "m")], writes=[(p, "m")])
        k.tt("dve", self.r[:, 0:1], self.m[:, 0:1], self.m[:, 0:1], ALU.mult,
             reads=[(p, "m")], writes=[(p, "r")])
        k.stt("dve", self.ms[:, 0:1], self.ms[:, 0:1], inv, self.r[:, 0:1], ALU.mult, ALU.subtract,
              reads=[(p, "ms"), (p, "r")], writes=[(p, "ms")])
        if self._eps_val != eps:
            k.memset("dve", self.eps[:, 0:1], eps, writes=[(p, "eps")])
            self._eps_val = eps
        k.act(self.r[:, 0:1], self.ms[:, 0:1], AF.Ln, bias=self.eps[:, 0:1],
              reads=[(p, "ms"), (p, "eps")], writes=[(p, "r")])
        k.act(self.r[:, 0:1], self.r[:, 0:1], AF.Exp, scale=-0.5, reads=[(p, "r")], writes=[(p, "r")])
        k.ts("dve", out_ap, z_ap, self.m[:, 0:1], self.r[:, 0:1], ALU.subtract, ALU.mult,
             reads=[zkey, (p, "m"), (p, "r")], writes=[outkey])



def host_consts_l0_attn(k):
    kpos = np.arange(S, dtype=np.float64)
    hi = np.floor(kpos / 256) * 256
    lo = kpos - hi
    kaug = np.stack([np.ones(S), hi, lo, hi, lo], 0)
    k.const("c_l0a_kaug", _bf16(kaug))
    k.const("c_l0a_qpos", _bf16(np.arange(S)[None, :].astype(np.float32)))
    m = np.zeros((128, 4, 512), np.float32)
    for p in range(4):
        kj = np.arange(128)[:, None]
        qi = np.arange(512)[None, :]
        m[:, p, :] = np.where(128 * p + kj <= qi, 0.0, -1e30)
    k.const("c_l0a_mask", _bf16(m))


def emit_l0_attn(k, io):
    P = "l0a"
    OC = k._sbufs["ones_col_b"]
    OR = k._sbufs["ones_row_f"]
    IDB = k._sbufs["ident_b"]
    WAVE = 16

    kaug = k.sbuf(P + "_kaug", (5, S), BF16)
    qpos = k.sbuf(P + "_qpos", (1, S), BF16)
    mask = k.sbuf(P + "_mask", (128, 4, 512), BF16)
    slope = k.sbuf(P + "_slope", (1, 4), F32)
    k.dma("sp", kaug[:, :], k._drams["c_l0a_kaug"].ap(), P + "c", writes=[(P, "kaug")])
    k.dma("sp", qpos[:, :], k._drams["c_l0a_qpos"].ap(), P + "c", writes=[(P, "qpos")])
    k.dma("sp", mask[:, :, :], k._drams["c_l0a_mask"].ap(), P + "c", writes=[(P, "mask")])
    k.dma("sp", slope[0:1, 0:1], io["slope"].ap(), P + "c", writes=[(P, "slope")])

    s_hi_b = k.sbuf(P + "_shb", (1, 1), BF16)
    k.copy("dve", s_hi_b[:, :], slope[0:1, 0:1], reads=[(P, "slope")], writes=[(P, "shb")])
    k.copy("dve", slope[0:1, 1:2], s_hi_b[:, :], reads=[(P, "shb")], writes=[(P, "slope")])
    k.tt("dve", slope[0:1, 2:3], slope[0:1, 0:1], slope[0:1, 1:2], ALU.subtract,
         reads=[(P, "slope")], writes=[(P, "slope")])
    k.ts("dve", slope[0:1, 3:4], slope[0:1, 0:1], -1.0, 0.0, ALU.mult, ALU.max,
         reads=[(P, "slope")], writes=[(P, "slope")])

    qaug = k.sbuf(P + "_qaug", (5, S), BF16)
    srow = k.sbuf(P + "_srow", (1, 5), F32)
    svec = k.sbuf(P + "_svec", (5, 1), F32)
    svec_d = k.dram(P + "_svec_d", (1, 5), F32)
    k.memset("dve", srow[0:1, 0:1], 0.0, writes=[(P, "srow")])
    for col, src_ in ((1, 1), (2, 1), (3, 2), (4, 2)):
        k.copy("dve", srow[0:1, col:col + 1], slope[0:1, src_:src_ + 1],
               reads=[(P, "slope")], writes=[(P, "srow")])
    k.dma("sp", svec_d.ap(), srow[0:1, :], P + "sv", reads=[(P, "srow")], writes=[(P, "svd")])
    k.dma("sp", svec[:, 0:1], svec_d.ap().rearrange("a b -> b a"), P + "sv",
          reads=[(P, "svd")], writes=[(P, "svec")])
    k.memset("dve", qaug[0:5, :], 0.0, writes=[(P, "qaug")])
    k.ts("dve", qaug[0:5, :], qaug[0:5, :], svec[0:5, 0:1], None, ALU.add,
         reads=[(P, "qaug"), (P, "svec")], writes=[(P, "qaug")])

    x_fm = load_fm(k, P + "_xfm", io["x_bf16"], P + "x")
    wqk = k.sbuf(P + "_wqk", (128, NCH, D), BF16)
    maxk = k.sbuf(P + "_maxk", (1, 1), F32)
    k.dma("pool", wqk[:, :, :], io["W_qk"].ap().rearrange("(c p) d -> p c d", p=128),
          P + "wqk", writes=[(P, "wqk")])

    sqt = k.sbuf(P + "_sqt", (128, S), BF16)
    pn = k.psum(P + "_pn", (1, S), F32)
    for kc in range(NCH):
        k.act(sqt[:, :], x_fm[:, kc, :], AF.Square, reads=[(P + "_xfm", kc)], writes=[(P, "sqt")])
        for sp_ in range(8):
            k.mm(pn[0:1, 512 * sp_:512 * (sp_ + 1)], OC[:, :], sqt[:, 512 * sp_:512 * (sp_ + 1)],
                 kc == 0, kc == NCH - 1, reads=[(P, "sqt"), ("ones_col_b",)], writes=[(P, "pn", sp_)])
    k.red(maxk[:, 0:1], pn[0:1, :], AX.X, ALU.max,
          reads=[(P, "pn", i) for i in range(8)], writes=[(P, "maxk")])
    k.act(maxk[:, 0:1], maxk[:, 0:1], AF.Ln, reads=[(P, "maxk")], writes=[(P, "maxk")])
    k.act(maxk[:, 0:1], maxk[:, 0:1], AF.Exp, scale=0.5, reads=[(P, "maxk")], writes=[(P, "maxk")])
    k.free(P + "_sqt", P + "_pn")

    v_dram = k.dram(P + "_v_dram", (S, D), BF16)
    wv_bf = k.dram(P + "_wv_bf", (D, D), BF16)
    k.dma("pool", wv_bf.ap(), io["W_v"].ap(), P + "wv", writes=[(P, "wvbf")])
    wvt = k.sbuf(P + "_wvt", (128, NCH, D), BF16)
    for c in range(NCH):
        k.dma("sp", wvt[:, c, :], wv_bf.ap()[:, 128 * c:128 * (c + 1)], P + "wvt",
              reads=[(P, "wvbf")], writes=[(P + "_wvt", c)], transpose=True)
    v_stage = k.sbuf(P + "_vstg", (128, 4, D), BF16)
    pV = [k.psum(P + "_pV0", (128, 512), F32), k.psum(P + "_pV1", (128, 512), F32)]
    for g in range(8):
        for j in range(4):
            tt_ = 4 * g + j
            for half in range(2):
                ps, pkey = pV[half], (P, "pV", half)
                for kc in range(NCH):
                    k.mm(ps[:, :], x_fm[:, kc, 128 * tt_:128 * (tt_ + 1)],
                         wvt[:, kc, 512 * half:512 * (half + 1)], kc == 0, kc == NCH - 1,
                         reads=[(P + "_xfm", kc), (P + "_wvt", kc)], writes=[pkey])
                k.copy("dve", v_stage[:, j, 512 * half:512 * (half + 1)], ps[:, :],
                       reads=[pkey], writes=[(P, "vstg", j)])
        k.dma("sp", v_dram.ap()[512 * g:512 * (g + 1), :].rearrange("(j p) d -> p j d", p=128),
              v_stage[:, :, :], P + "vout", reads=[(P, "vstg", jj) for jj in range(4)],
              writes=[(P, "vdram", g)])
    k.free(P + "_wvt", P + "_vstg", P + "_pV0", P + "_pV1")

    q_qc = k.sbuf(P + "_qqc", (128, NCH, 512), BF16)
    sqq = k.sbuf(P + "_sqq", (128, 512), BF16)
    nq = k.sbuf(P + "_nq", (1, 512), F32)
    brow = k.sbuf(P + "_brow", (1, 512), F32)
    pt = k.sbuf(P + "_pt", (128, WAVE, 512), BF16)
    v_buf = k.sbuf(P + "_vbuf", (128, WAVE, D), BF16)
    o_acc = k.sbuf(P + "_oacc", (128, NCH, 512), F32)
    o_d = k.sbuf(P + "_od", (128, 128), BF16)
    rsum = k.sbuf(P + "_rsum", (1, 512), F32)
    rb = k.sbuf(P + "_rb", (128, 512), F32)
    xt = k.sbuf(P + "_xt", (128, D), F32)
    yt = k.sbuf(P + "_yt", (128, D), F32)
    pQ = k.psum(P + "_pQ", (128, 512), F32)
    pS = k.psum(P + "_pS", (128, 512), F32)
    pSum = k.psum(P + "_pSum", (1, 512), F32)
    pO = [k.psum(P + "_pO0", (128, 512), F32), k.psum(P + "_pO1", (128, 512), F32)]
    pR = k.psum(P + "_pR", (128, 512), F32)
    pT = k.psum(P + "_pT", (128, 128), BF16)
    ln = LNKit(k, P + "_ln")

    for qc in range(8):
        qsl = slice(512 * qc, 512 * (qc + 1))
        for fc in range(NCH):
            for kc in range(NCH):
                k.mm(pQ[:, :], wqk[:, kc, 128 * fc:128 * (fc + 1)], x_fm[:, kc, qsl],
                     kc == 0, kc == NCH - 1, reads=[(P, "wqk"), (P + "_xfm", kc)], writes=[(P, "pQ")])
            k.act(q_qc[:, fc, :], pQ[:, :], AF.Copy, scale=1.0 / 32.0,
                  reads=[(P, "pQ")], writes=[(P, "qqc", fc)])
        for fc in range(NCH):
            k.act(sqq[:, :], q_qc[:, fc, :], AF.Square, reads=[(P, "qqc", fc)], writes=[(P, "sqq")])
            k.mm(pSum[0:1, :], OC[:, :], sqq[:, :], fc == 0, fc == NCH - 1,
                 reads=[(P, "sqq"), ("ones_col_b",)], writes=[(P, "pSum")])
        k.copy("dve", nq[:, :], pSum[0:1, :], reads=[(P, "pSum")], writes=[(P, "nq")])
        k.act(nq[:, :], nq[:, :], AF.Ln, reads=[(P, "nq")], writes=[(P, "nq")])
        k.act(nq[:, :], nq[:, :], AF.Exp, scale=0.5, reads=[(P, "nq")], writes=[(P, "nq")])
        k.ts("dve", brow[:, :], nq[:, :], maxk[0:1, 0:1], None, ALU.mult,
             reads=[(P, "nq"), (P, "maxk")], writes=[(P, "brow")])
        k.stt("dve", brow[:, :], qpos[0:1, qsl], slope[0:1, 3:4], brow[:, :], ALU.mult, ALU.add,
              reads=[(P, "qpos"), (P, "slope"), (P, "brow")], writes=[(P, "brow")])
        k.stt("dve", brow[:, :], qpos[0:1, qsl], slope[0:1, 0:1], brow[:, :], ALU.mult, ALU.add,
              reads=[(P, "qpos"), (P, "slope"), (P, "brow")], writes=[(P, "brow")])
        k.ts("dve", qaug[0:1, qsl], brow[:, :], -1.0, None, ALU.mult,
             reads=[(P, "brow")], writes=[(P, "qaug")])

        kc_count = 4 * (qc + 1)
        waves = [range(w, min(w + WAVE, kc_count)) for w in range(0, kc_count, WAVE)]
        for wi, kcs in enumerate(waves):
            nkc = len(kcs)
            k.dma("sp", v_buf[:, 0:nkc, :],
                  v_dram.ap()[128 * kcs[0]:128 * (kcs[-1] + 1), :].rearrange("(j p) d -> p j d", p=128),
                  P + "vin", reads=[(P, "vdram", g) for g in range(kcs[0] // 4, (kcs[-1] + 4) // 4)],
                  writes=[(P, "vbuf")])
            for kc in kcs:
                ksl = slice(128 * kc, 128 * (kc + 1))
                for c in range(NCH):
                    k.mm(pS[:, :], x_fm[:, c, ksl], q_qc[:, c, :], c == 0, False,
                         reads=[(P + "_xfm", c), (P, "qqc", c)], writes=[(P, "pS")])
                k.mm(pS[:, :], kaug[:, ksl], qaug[:, qsl], False, True,
                     reads=[(P, "kaug"), (P, "qaug")], writes=[(P, "pS")])
                p_ = kc - 4 * qc
                if p_ >= 0:
                    k.tt("dve", pS[:, :], pS[:, :], mask[:, p_, :], ALU.add,
                         reads=[(P, "pS"), (P, "mask")], writes=[(P, "pS")])
                k.act(pt[:, kc % WAVE, :], pS[:, :], AF.Exp, reads=[(P, "pS")], writes=[(P, "pt", kc % WAVE)])
                k.mm(pSum[0:1, :], OC[:, :], pt[:, kc % WAVE, :], kc == 0, kc == kc_count - 1,
                     reads=[(P, "pt", kc % WAVE), ("ones_col_b",)], writes=[(P, "pSum")])
            for dvc in range(NCH):
                ps, pkey = pO[dvc % 2], (P, "pO", dvc % 2)
                for j, kc in enumerate(kcs):
                    k.mm(ps[:, :], v_buf[:, kc % WAVE, 128 * dvc:128 * (dvc + 1)], pt[:, kc % WAVE, :],
                         j == 0, kc == kcs[-1],
                         reads=[(P, "vbuf"), (P, "pt", kc % WAVE)], writes=[pkey])
                if wi == 0:
                    k.copy("dve", o_acc[:, dvc, :], ps[:, :], reads=[pkey], writes=[(P, "oacc", dvc)])
                else:
                    k.tt("dve", o_acc[:, dvc, :], o_acc[:, dvc, :], ps[:, :], ALU.add,
                         reads=[pkey, (P, "oacc", dvc)], writes=[(P, "oacc", dvc)])

        k.recip(rsum[:, :], pSum[0:1, :], reads=[(P, "pSum")], writes=[(P, "rsum")])
        k.mm(pR[:, :], OR[:, :], rsum[:, :], True, True,
             reads=[(P, "rsum"), ("ones_row_f",)], writes=[(P, "pR")])
        k.copy("dve", rb[:, :], pR[:, :], reads=[(P, "pR")], writes=[(P, "rb")])
        for tq in range(4):
            trow = 512 * qc + 128 * tq
            tsl = slice(128 * tq, 128 * (tq + 1))
            k.dma("sp", xt[:, :], io["x_f32"].ap()[trow:trow + 128, :], P + "xt", writes=[(P, "xt")])
            for dvc in range(NCH):
                k.tt("dve", o_d[:, :], o_acc[:, dvc, tsl], rb[:, tsl], ALU.mult,
                     reads=[(P, "oacc", dvc), (P, "rb")], writes=[(P, "od")])
                k.transp(pT[:, :], o_d[:, :], IDB[:, :],
                         reads=[(P, "od"), ("ident_b",)], writes=[(P, "pT")])
                k.tt("dve", xt[:, 128 * dvc:128 * (dvc + 1)], xt[:, 128 * dvc:128 * (dvc + 1)],
                     pT[:, :], ALU.add, reads=[(P, "xt"), (P, "pT")], writes=[(P, "xt")])
            ln.apply(xt[:, :], yt[:, :], (P, "xt"), (P, "yt"))
            k.dma("sp", io["y_f32"].ap()[trow:trow + 128, :], yt[:, :], P + "y", reads=[(P, "yt")])
            k.dma("pool", io["y_bf16"].ap()[trow:trow + 128, :], yt[:, :], P + "yb", reads=[(P, "yt")])

    ln.free()
    k.free(P + "_kaug", P + "_qpos", P + "_mask", P + "_slope", P + "_shb", P + "_qaug",
           P + "_srow", P + "_svec",
           P + "_xfm", P + "_wqk", P + "_maxk", P + "_qqc",
           P + "_sqq", P + "_nq", P + "_brow", P + "_pt", P + "_vbuf", P + "_oacc", P + "_od",
           P + "_rsum", P + "_rb", P + "_xt", P + "_yt",
           P + "_pQ", P + "_pS", P + "_pSum", P + "_pO0", P + "_pO1", P + "_pR", P + "_pT")



def emit_l2_ffn(k, io):
    P = "l2f"

    mu = k.sbuf(P + "_mu", (128, 4, NCH), F32)
    k.dma("sp", mu[:, 0, :], io["mu_r"].ap(), P + "c", writes=[(P, "mu")])
    k.dma("sp", mu[:, 2, :], io["mu_k"].ap(), P + "c", writes=[(P, "mu")])
    for i in (0, 2):
        k.ts("dve", mu[:, i + 1, :], mu[:, i, :], -1.0, 1.0, ALU.mult, ALU.add,
             reads=[(P, "mu")], writes=[(P, "mu")])

    xk = k.sbuf(P + "_xk", (128, NCH, S), BF16)
    xr = k.sbuf(P + "_xr", (128, NCH, S), BF16)
    x_fm = load_fm(k, P + "_xfm", io["x_bf16"], P + "x")
    for c in range(NCH):
        for (dst, dkey, im, iom) in ((xk, "xk", 2, 3), (xr, "xr", 0, 1)):
            k.ts("dve", dst[:, c, :], x_fm[:, c, :], mu[:, im, c:c + 1], None, ALU.mult,
                 reads=[(P + "_xfm", c), (P, "mu")], writes=[(P, dkey, c)])
            k.stt("dve", dst[:, c, 1:S], x_fm[:, c, 0:S - 1], mu[:, iom, c:c + 1],
                  dst[:, c, 1:S], ALU.mult, ALU.add,
                  reads=[(P + "_xfm", c), (P, "mu"), (P, dkey, c)], writes=[(P, dkey, c)])
    k.free(P + "_xfm")

    wkt = load_wT(k, P + "_wkt", io["W_k"], P + "wk")
    kstg = k.sbuf(P + "_kstg", (128, NCH, 512), BF16)
    pA = k.psum(P + "_pA", (128, 512), F32)
    pB = k.psum(P + "_pB", (128, 512), F32)
    for sp_ in range(8):
        ssl = slice(512 * sp_, 512 * (sp_ + 1))
        for e in range(NCH):
            ps, pkey = (pA, (P, "pA")) if e % 2 == 0 else (pB, (P, "pB"))
            for c in range(NCH):
                k.mm(ps[:, :], wkt[:, c, 128 * e:128 * (e + 1)], xk[:, c, ssl],
                     c == 0, c == NCH - 1,
                     reads=[(P + "_wkt", c), (P, "xk", c)], writes=[pkey])
            k.ts("dve", kstg[:, e, :], ps[:, :], 0.0, None, ALU.max,
                 reads=[pkey], writes=[(P, "kstg", e)])
        for e in range(NCH):
            k.act(xk[:, e, ssl], kstg[:, e, :], AF.Square,
                  reads=[(P, "kstg", e)], writes=[(P, "xk", e)])
    k.free(P + "_wkt", P + "_kstg")

    wrt = load_wT(k, P + "_wrt", io["W_r"], P + "wr")
    wvt = load_wT(k, P + "_wvt", io["W_v"], P + "wv")
    rstg = k.sbuf(P + "_rstg", (128, 512), BF16)
    zt = k.sbuf(P + "_zt", (128, D), F32)
    yt = k.sbuf(P + "_yt", (128, D), F32)
    ln = LNKit(k, P + "_ln")
    for tt in range(32):
        tsl = slice(128 * tt, 128 * (tt + 1))
        k.dma("sp", zt[:, :], io["x_f32"].ap()[tsl, :], P + "xt", writes=[(P, "zt")])
        for half in range(2):
            hsl = slice(512 * half, 512 * (half + 1))
            for c in range(NCH):
                k.mm(pA[:, :], xr[:, c, tsl], wrt[:, c, hsl], c == 0, c == NCH - 1,
                     reads=[(P + "_wrt", c), (P, "xr", c)], writes=[(P, "pA")])
            k.act(rstg[:, :], pA[:, :], AF.Sigmoid, reads=[(P, "pA")], writes=[(P, "rstg")])
            for c in range(NCH):
                k.mm(pB[:, :], xk[:, c, tsl], wvt[:, c, hsl], c == 0, c == NCH - 1,
                     reads=[(P + "_wvt", c), (P, "xk", c)], writes=[(P, "pB")])
            k.tt("dve", rstg[:, :], rstg[:, :], pB[:, :], ALU.mult,
                 reads=[(P, "rstg"), (P, "pB")], writes=[(P, "rstg")])
            k.tt("dve", zt[:, hsl], zt[:, hsl], rstg[:, :], ALU.add,
                 reads=[(P, "zt"), (P, "rstg")], writes=[(P, "zt")])
        ln.apply(zt[:, :], yt[:, :], (P, "zt"), (P, "yt"))
        k.dma("sp", io["y_f32"].ap()[tsl, :], yt[:, :], P + "y", reads=[(P, "yt")])
        k.dma("pool", io["y_bf16"].ap()[tsl, :], yt[:, :], P + "yb", reads=[(P, "yt")])
    ln.free()
    k.free(P + "_mu", P + "_xk", P + "_xr", P + "_wrt", P + "_wvt", P + "_rstg",
           P + "_zt", P + "_yt", P + "_pA", P + "_pB")



def emit_l3_attn(k, io, upto=None):
    P = "l3a"

    def _dump(name, ap, reads):
        dbg = k.dram("dbg_" + name, [x for x in ap.shape], BF16 if "bf" in name else F32,
                     kind="ExternalOutput") if False else None
    OR = k._sbufs["ones_row_f"]
    IDB = k._sbufs["ident_b"]
    NH, DH = 16, 64

    def phi(buf, key, cols, tmp, tmpkey):
        k.ts("dve", tmp, buf, 0.0, None, ALU.min, reads=[key], writes=[tmpkey])
        k.act(tmp, tmp, AF.Exp, reads=[tmpkey], writes=[tmpkey])
        k.stt("dve", buf, buf, 0.0, tmp, ALU.max, ALU.add,
              reads=[key, tmpkey], writes=[key])

    qf = k.sbuf(P + "_qf", (128, NCH, S), BF16)
    kva = k.sbuf(P + "_kva", (128, 8, 128), BF16)
    bqc = k.sbuf(P + "_bqc", (128, NCH), F32)
    bkvb = k.sbuf(P + "_bkvb", (128, 2, D), F32)
    bob = k.sbuf(P + "_bob", (128, D), F32)
    k.dma("sp", bqc[:, :], io["bq_col"].ap(), P + "c", writes=[(P, "bqc")])
    pBC = k.psum(P + "_pBC", (128, 512), F32)
    for i, nm in enumerate(("bk_row", "bv_row")):
        for hh in range(2):
            k.dma("sp", bkvb[0:1, i, 512 * hh:512 * (hh + 1)], io[nm].ap()[0:1, 512 * hh:512 * (hh + 1)],
                  P + "c", writes=[(P, "brow", i)])
    for i in range(2):
        for hh in range(2):
            k.mm(pBC[:, :], OR[:, :], bkvb[0:1, i, 512 * hh:512 * (hh + 1)], True, True,
                 reads=[(P, "brow", i), ("ones_row_f",)], writes=[(P, "pBC")])
            k.copy("dve", bkvb[:, i, 512 * hh:512 * (hh + 1)], pBC[:, :],
                   reads=[(P, "pBC")], writes=[(P, "brow", i)])
    for hh in range(2):
        k.dma("sp", bob[0:1, 512 * hh:512 * (hh + 1)], io["bo_row"].ap()[0:1, 512 * hh:512 * (hh + 1)],
              P + "c", writes=[(P, "bob")])
        k.mm(pBC[:, :], OR[:, :], bob[0:1, 512 * hh:512 * (hh + 1)], True, True,
             reads=[(P, "bob"), ("ones_row_f",)], writes=[(P, "pBC")])
        k.copy("dve", bob[:, 512 * hh:512 * (hh + 1)], pBC[:, :],
               reads=[(P, "pBC")], writes=[(P, "bob")])

    tmp = k.sbuf(P + "_tmp", (128, S), BF16)
    pA = k.psum(P + "_pA", (128, 512), F32)
    pB = k.psum(P + "_pB", (128, 512), F32)
    x_fm = load_fm(k, P + "_xfm", io["x_bf16"], P + "x")

    wqt = load_wT(k, P + "_wqt", io["Wq"], P + "wq")
    for fc in range(NCH):
        for sp_ in range(8):
            ps, pk_ = (pA, (P, "pA")) if sp_ % 2 == 0 else (pB, (P, "pB"))
            for c in range(NCH):
                k.mm(ps[:, :], wqt[:, c, 128 * fc:128 * (fc + 1)], x_fm[:, c, 512 * sp_:512 * (sp_ + 1)],
                     c == 0, c == NCH - 1, reads=[(P + "_wqt", c), (P + "_xfm", c)], writes=[pk_])
            k.ts("dve", qf[:, fc, 512 * sp_:512 * (sp_ + 1)], ps[:, :], bqc[:, fc:fc + 1],
                 None, ALU.add, reads=[pk_, (P, "bqc")], writes=[(P, "qf", fc)])
        phi(qf[:, fc, :], (P, "qf", fc), S, tmp[:, :], (P, "tmp"))
    if upto == "qf":
        dbg = k.dram("dbg", (128, NCH, S), BF16, kind="ExternalOutput")
        k.dma("sp", dbg.ap(), qf[:, :, :], P + "dbg", reads=[(P, "qf", c) for c in range(NCH)])
        return
    k.free(P + "_wqt")

    wkt = load_wT(k, P + "_wkt", io["Wk"], P + "wk")
    wvt = load_wT(k, P + "_wvt", io["Wv"], P + "wv")
    kstg = k.sbuf(P + "_kstg", (128, D), BF16)
    vstg = k.sbuf(P + "_vstg", (128, NH, 65), BF16)
    ktmp = k.sbuf(P + "_ktmp", (128, D), BF16)
    k.memset("dve", vstg[:, :, 64:65], 1.0, writes=[(P, "vones")])
    kvps = [k.psum(P + "_kv0", (128, 4, 128), F32), k.psum(P + "_kv1", (128, 4, 128), F32)]
    for j in range(2):
        psum_zero(k, kvps[j][:, :, :].rearrange("p a b -> p (a b)"),
                  [(P, "kv", h) for h in range(8 * j, 8 * j + 8)])
    for tt in range(32):
        tsl = slice(128 * tt, 128 * (tt + 1))
        for half in range(2):
            hsl = slice(512 * half, 512 * (half + 1))
            for c in range(NCH):
                k.mm(pA[:, :], x_fm[:, c, tsl], wkt[:, c, hsl], c == 0, c == NCH - 1,
                     reads=[(P + "_wkt", c), (P + "_xfm", c)], writes=[(P, "pA")])
            k.tt("dve", kstg[:, hsl], pA[:, :], bkvb[:, 0, hsl], ALU.add,
                 reads=[(P, "pA"), (P, "brow", 0)], writes=[(P, "kstg")])
            for c in range(NCH):
                k.mm(pB[:, :], x_fm[:, c, tsl], wvt[:, c, hsl], c == 0, c == NCH - 1,
                     reads=[(P + "_wvt", c), (P + "_xfm", c)], writes=[(P, "pB")])
            k.stt("dve", vstg[:, 8 * half:8 * (half + 1), 0:64],
                  pB[:, :].rearrange("p (a b) -> p a b", a=8), 1.0,
                  bkvb[:, 1, hsl].rearrange("p (a b) -> p a b", a=8), ALU.mult, ALU.add,
                  reads=[(P, "pB"), (P, "brow", 1)], writes=[(P, "vstg", half)])
        phi(kstg[:, :], (P, "kstg"), D, ktmp[:, :], (P, "ktmp"))
        for h in range(NH):
            ps = kvps[h // 8]
            slot = (h % 8) // 2
            pbase = 64 * (h % 2)
            k.mm(ps[pbase:pbase + 64, slot, 0:65], kstg[:, 64 * h:64 * (h + 1)],
                 vstg[:, h, 0:65], False, False,
                 reads=[(P, "kstg"), (P, "vstg", h // 8), (P, "vones")],
                 writes=[(P, "kv", h)])
    for j in range(2):
        psum_close(k, kvps[j][:, :, :].rearrange("p a b -> p (a b)"),
                   [(P, "kv", h) for h in range(8 * j, 8 * j + 8)])
    for j in range(2):
        k.copy("dve", kva[:, 4 * j:4 * (j + 1), :], kvps[j][:, :, :],
               reads=[(P, "kv", h) for h in range(8 * j, 8 * j + 8)], writes=[(P, "kva")])
    k.free(P + "_wkt", P + "_wvt", P + "_kstg", P + "_vstg", P + "_ktmp",
           P + "_kv0", P + "_kv1", P + "_xfm")

    if upto == "kv":
        dbg = k.dram("dbg", (128, 8, 128), BF16, kind="ExternalOutput")
        k.dma("sp", dbg.ap(), kva[:, :, :], P + "dbg", reads=[(P, "kva")])
        return
    o_fm = k.sbuf(P + "_ofm", (128, NCH, S), BF16)
    ops_ = [k.psum(P + "_po0", (128, 4, 128), F32), k.psum(P + "_po1", (128, 4, 128), F32),
            k.psum(P + "_po2", (128, 4, 128), F32), k.psum(P + "_po3", (128, 4, 128), F32)]
    nrm = k.sbuf(P + "_nrm", (128, NH), F32)
    otmp = k.sbuf(P + "_otmp", (128, NH, 64), BF16)
    pT = k.psum(P + "_pT", (128, 128), BF16)
    import os as _os0
    _ttmax = int(_os0.environ.get("L3A_TTMAX", "32"))
    _oops = set((_os0.environ.get("L3A_OPS") or "mm,div,tr").split(","))
    for tt in range(_ttmax):
        tsl = slice(128 * tt, 128 * (tt + 1))
        if "mm" in _oops:
            for h in range(int(_os0.environ.get("L3A_HMAX", str(NH)))):
                j = (h % 2) + 2 * (h // 8)
                slot = (h % 8) // 2
                ps = ops_[j]
                lhs = qf[:, h // 2, :][64 * (h % 2):64 * (h % 2) + 64, tsl]
                kslice = kva[64 * (h % 2):64 * (h % 2) + 64, (h % 8) // 2 + 4 * (h // 8), 0:65]
                k.mm(ps[:, slot, 0:65], lhs, kslice, True, True,
                     reads=[(P, "qf", h // 2), (P, "kva")], writes=[(P, "po", j)])
        ot5 = otmp[:, :, :].rearrange("p (g s q) c -> p g s q c", g=2, q=2)
        nr4 = nrm[:, :].rearrange("p (g s q) -> p g s q", g=2, q=2)
        if "div" not in _oops:
            for j in range(4):
                k.copy("dve", ot5[:, j // 2, :, j % 2, :], ops_[j][:, :, 0:64],
                       reads=[(P, "po", j)], writes=[(P, "otmp", j)])
        else:
            for j in range(4):
                k.copy("dve", bcast_free(nr4[:, j // 2, :, j % 2], 1), ops_[j][:, :, 64:65],
                       reads=[(P, "po", j)], writes=[(P, "nrm")])
            k.ts("dve", nrm[:, :], nrm[:, :], 1e-6, None, ALU.max, reads=[(P, "nrm")], writes=[(P, "nrm")])
            k.recip(nrm[:, :], nrm[:, :], reads=[(P, "nrm")], writes=[(P, "nrm")])
            for j in range(4):
                k.tt("dve", ot5[:, j // 2, :, j % 2, :], ops_[j][:, :, 0:64],
                     bcast_free(nr4[:, j // 2, :, j % 2], 64), ALU.mult,
                     reads=[(P, "po", j), (P, "nrm")], writes=[(P, "otmp", j)])
        if "tr" not in _oops:
            for c in range(NCH):
                k.copy("dve", o_fm[:, c, tsl], otmp[:, :, :].rearrange("p a b -> p (a b)")[:, 128 * c:128 * (c + 1)],
                       reads=[(P, "otmp", c // 2)], writes=[(P, "ofm", c)])
        else:
            for c in range(NCH):
                k.transp(pT[:, :], otmp[:, :, :].rearrange("p a b -> p (a b)")[:, 128 * c:128 * (c + 1)],
                         IDB[:, :], reads=[(P, "otmp", c // 2), ("ident_b",)], writes=[(P, "pT")])
                k.copy("dve", o_fm[:, c, tsl], pT[:, :], reads=[(P, "pT")], writes=[(P, "ofm", c)])
    if upto == "ofm":
        dbg = k.dram("dbg", (128, NCH, S), BF16, kind="ExternalOutput")
        k.dma("sp", dbg.ap(), o_fm[:, :, :], P + "dbg", reads=[(P, "ofm", c) for c in range(NCH)])
        return
    wot = load_wT(k, P + "_wot", io["Wo"], P + "wo")
    zt = k.sbuf(P + "_zt", (128, D), F32)
    yt = k.sbuf(P + "_yt", (128, D), F32)
    ln = LNKit(k, P + "_ln")
    for tt in range(32):
        tsl = slice(128 * tt, 128 * (tt + 1))
        k.dma("sp", zt[:, :], io["x_f32"].ap()[tsl, :], P + "xt", writes=[(P, "zt")])
        for half in range(2):
            hsl = slice(512 * half, 512 * (half + 1))
            ps, pk_ = (pA, (P, "pA")) if half == 0 else (pB, (P, "pB"))
            for c in range(NCH):
                k.mm(ps[:, :], o_fm[:, c, tsl], wot[:, c, hsl], c == 0, c == NCH - 1,
                     reads=[(P + "_wot", c), (P, "ofm", c)], writes=[pk_])
            k.tt("dve", zt[:, hsl], zt[:, hsl], ps[:, :], ALU.add,
                 reads=[(P, "zt"), pk_], writes=[(P, "zt")])
        k.tt("dve", zt[:, :], zt[:, :], bob[:, :], ALU.add,
             reads=[(P, "zt"), (P, "bob")], writes=[(P, "zt")])
        ln.apply(zt[:, :], yt[:, :], (P, "zt"), (P, "yt"))
        k.dma("sp", io["y_f32"].ap()[tsl, :], yt[:, :], P + "y", reads=[(P, "yt")])
        k.dma("pool", io["y_bf16"].ap()[tsl, :], yt[:, :], P + "yb", reads=[(P, "yt")])
    ln.free()
    k.free(P + "_qf", P + "_kva", P + "_bqc", P + "_bkvb", P + "_bob", P + "_tmp",
           P + "_ofm", P + "_wot", P + "_zt", P + "_yt", P + "_pA", P + "_pB",
           P + "_pBC", P + "_pT", P + "_po0", P + "_po1", P + "_po2", P + "_po3",
           P + "_nrm", P + "_otmp")



def host_consts_l2_attn(k):
    NH, HD = 16, 64
    pos = np.arange(S, dtype=np.float64)
    i_idx = (np.arange(128) % HD) // 2
    inv_freq = 10000.0 ** (-2.0 * i_idx / HD)
    ang = pos[None, :] * inv_freq[:, None]
    k.const("c_l2a_cq", _bf16(np.cos(ang) / 8.0))
    k.const("c_l2a_sq", _bf16(np.sin(ang) / 8.0))
    k.const("c_l2a_ck", _bf16(np.cos(ang)))
    k.const("c_l2a_sk", _bf16(np.sin(ang)))
    Pm = np.zeros((128, 128), np.float64)
    for d in range(128):
        if d % 2 == 0:
            Pm[d, d + 1] = -1.0
        else:
            Pm[d, d - 1] = 1.0
    k.const("c_l2a_rotT", _bf16(Pm.T))
    E = np.zeros((128, NCH, NH), np.float64)
    for c in range(NCH):
        E[0:64, c, 2 * c] = 1.0
        E[64:128, c, 2 * c + 1] = 1.0
    k.const("c_l2a_E", _bf16(E))


def emit_l2_attn(k, io):
    P = "l2a"
    OC = k._sbufs["ones_col_b"]
    OR = k._sbufs["ones_row_f"]
    NH, HD = 16, 64
    WAVE = 16

    q_fm = k.sbuf(P + "_qfm", (128, NCH, S), BF16)
    k_dram = k.dram(P + "_k_dram", (D, S), BF16)
    v_dram = k.dram(P + "_v_dram", (S, NH * 65), BF16)
    o_dram = k.dram(P + "_o_dram", (D, S), BF16)
    nrm2 = k.sbuf(P + "_nrm2", (16, 512), F32)
    nmax = k.sbuf(P + "_nmax", (16, 1), F32)
    negb = k.sbuf(P + "_negb", (128, NH), F32)
    bcol = k.sbuf(P + "_bcol", (16, 2), F32)
    brow = k.sbuf(P + "_brow", (1, 16), F32)
    brow_d = k.dram(P + "_brow_d", (16, 1), F32)
    emap = k.sbuf(P + "_emap", (128, NCH, NH), BF16)
    k.dma("sp", emap[:, :, :], k._drams["c_l2a_E"].ap(), P + "c", writes=[(P, "emap")])

    pA = k.psum(P + "_pA", (128, 512), F32)
    pB = k.psum(P + "_pB", (128, 512), F32)
    x_fm = load_fm(k, P + "_xfm", io["x_bf16"], P + "x")
    pN = k.psum(P + "_pN", (128, 512), F32)

    rotT = k.sbuf(P + "_rotT", (128, 128), BF16)
    ctab = k.sbuf(P + "_ctab", (128, 2, S), BF16)
    stab = k.sbuf(P + "_stab", (128, 2, S), BF16)
    t0s = k.sbuf(P + "_t0s", (128, 512), BF16)
    t1s = k.sbuf(P + "_t1s", (128, 512), BF16)
    kstg = k.sbuf(P + "_kstg", (128, S), BF16)
    k.dma("sp", rotT[:, :], k._drams["c_l2a_rotT"].ap(), P + "c", writes=[(P, "rotT")])
    k.dma("sp", ctab[:, 0, :], k._drams["c_l2a_cq"].ap(), P + "c", writes=[(P, "ctab")])
    k.dma("sp", ctab[:, 1, :], k._drams["c_l2a_ck"].ap(), P + "c", writes=[(P, "ctab")])
    k.dma("sp", stab[:, 0, :], k._drams["c_l2a_sq"].ap(), P + "c", writes=[(P, "stab")])
    k.dma("sp", stab[:, 1, :], k._drams["c_l2a_sk"].ap(), P + "c", writes=[(P, "stab")])

    for ti, (wname, is_q) in enumerate((("Wq", True), ("Wk", False))):
        wtn = P + "_wt" + wname
        wt = load_wT(k, wtn, io[wname], P + "w" + wname)
        for fc in range(NCH):
            for sp_ in range(8):
                ssl = slice(512 * sp_, 512 * (sp_ + 1))
                dst = q_fm[:, fc, ssl] if is_q else kstg[:, ssl]
                dkey = (P, "qfm", fc) if is_q else (P, "kstg")
                for c in range(NCH):
                    k.mm(pA[:, :], wt[:, c, 128 * fc:128 * (fc + 1)], x_fm[:, c, ssl],
                         c == 0, c == NCH - 1, reads=[(wtn, c), (P + "_xfm", c)], writes=[(P, "pA")])
                k.copy("dve", t0s[:, :], pA[:, :], reads=[(P, "pA")], writes=[(P, "t0s")])
                k.mm(pB[:, :], rotT[:, :], t0s[:, :], True, True,
                     reads=[(P, "rotT"), (P, "t0s")], writes=[(P, "pB")])
                k.tt("dve", dst, pB[:, :], stab[:, ti, ssl], ALU.mult,
                     reads=[(P, "pB"), (P, "stab")], writes=[dkey])
                k.tt("pool", t1s[:, :], t0s[:, :], ctab[:, ti, ssl], ALU.mult,
                     reads=[(P, "t0s"), (P, "ctab")], writes=[(P, "t1s")])
                k.tt("dve", dst, dst, t1s[:, :], ALU.add,
                     reads=[dkey, (P, "t1s")], writes=[dkey])
                k.act(t0s[:, :], dst, AF.Square, reads=[dkey], writes=[(P, "t0s")])
                k.mm(pN[0:16, :], emap[:, fc, :], t0s[:, :], True, True,
                     reads=[(P, "t0s"), (P, "emap")], writes=[(P, "pN")])
                k.copy("dve", nrm2[:, :], pN[0:16, :], reads=[(P, "pN")], writes=[(P, "nrm2")])
                k.red(nmax[:, 0:1], nrm2[:, :], AX.X, ALU.max, reads=[(P, "nrm2")], writes=[(P, "nmax")])
                if fc == 0 and sp_ == 0:
                    k.copy("dve", bcol[:, ti:ti + 1], nmax[:, 0:1], reads=[(P, "nmax")], writes=[(P, "bcol")])
                else:
                    k.tt("dve", bcol[:, ti:ti + 1], bcol[:, ti:ti + 1], nmax[:, 0:1], ALU.max,
                         reads=[(P, "nmax"), (P, "bcol")], writes=[(P, "bcol")])
            if not is_q:
                k.dma("sp", k_dram.ap()[128 * fc:128 * (fc + 1), :], kstg[:, :], P + "kout",
                      reads=[(P, "kstg")], writes=[(P, "kdram", fc)])
        k.free(wtn)

    wt = load_wT(k, P + "_wtv", io["Wv"], P + "wWv")
    vstg = k.sbuf(P + "_vstg", (128, NH, 65), BF16)
    k.memset("dve", vstg[:, :, 64:65], 1.0, writes=[(P, "vones")])
    for tt in range(32):
        tsl = slice(128 * tt, 128 * (tt + 1))
        for half in range(2):
            hsl = slice(512 * half, 512 * (half + 1))
            ps, pk_ = (pA, (P, "pA")) if half == 0 else (pB, (P, "pB"))
            for c in range(NCH):
                k.mm(ps[:, :], x_fm[:, c, tsl], wt[:, c, hsl], c == 0, c == NCH - 1,
                     reads=[(P + "_wtv", c), (P + "_xfm", c)], writes=[pk_])
            k.copy("dve", vstg[:, 8 * half:8 * (half + 1), 0:64],
                   ps[:, :].rearrange("p (a b) -> p a b", a=8),
                   reads=[pk_], writes=[(P, "vstg", half)])
        k.dma("sp", v_dram.ap()[tsl, :], vstg[:, :, :].rearrange("p a b -> p (a b)"),
              P + "vout", reads=[(P, "vstg", 0), (P, "vstg", 1), (P, "vones")],
              writes=[(P, "vdram", tt)])
    k.free(P + "_wtv", P + "_vstg")

    k.tt("dve", bcol[:, 0:1], bcol[:, 0:1], bcol[:, 1:2], ALU.mult,
         reads=[(P, "bcol")], writes=[(P, "bcol")])
    k.act(bcol[:, 0:1], bcol[:, 0:1], AF.Ln, reads=[(P, "bcol")], writes=[(P, "bcol")])
    k.act(bcol[:, 0:1], bcol[:, 0:1], AF.Exp, scale=0.5, reads=[(P, "bcol")], writes=[(P, "bcol")])
    k.ts("dve", bcol[:, 1:2], bcol[:, 0:1], -1.0, None, ALU.mult, reads=[(P, "bcol")], writes=[(P, "bcol")])
    k.dma("sp", brow_d.ap(), bcol[:, 1:2], P + "br",
          reads=[(P, "bcol")], writes=[(P, "browd")])
    k.dma("sp", brow[:, :], brow_d.ap().rearrange("a b -> b a"), P + "br",
          reads=[(P, "browd")], writes=[(P, "brow")])
    k.mm(pA[:, 0:16], OR[:, :], brow[:, :], True, True,
         reads=[(P, "brow"), ("ones_row_f",)], writes=[(P, "pA")])
    k.copy("dve", negb[:, :], pA[:, 0:16], reads=[(P, "pA")], writes=[(P, "negb")])
    k.free(P + "_rotT", P + "_ctab", P + "_stab", P + "_t0s", P + "_t1s", P + "_kstg",
           P + "_xfm", P + "_pN")

    mask = k.sbuf(P + "_mask", (128, 4, 512), BF16)
    k.dma("sp", mask[:, :, :], k._drams["c_l0a_mask"].ap(), P + "c2", writes=[(P, "mask")])
    k_wave = k.sbuf(P + "_kwave", (128, NCH, WAVE * 128), BF16)
    v_wave = k.sbuf(P + "_vwave", (128, WAVE, NH * 65), BF16)
    pt = k.sbuf(P + "_pt", (128, WAVE, 512), BF16)
    o_acc = k.sbuf(P + "_oacc", (128, NH, 512), F32)
    o_stg = k.sbuf(P + "_ostg", (64, NH, 512), BF16)
    rn = k.sbuf(P + "_rn", (1, 512), F32)
    rb = k.sbuf(P + "_rb", (64, 512), F32)
    pS = [k.psum(P + f"_pS{i}", (128, 512), F32) for i in range(4)]
    pO = [k.psum(P + "_pO0", (128, 512), F32), k.psum(P + "_pO1", (128, 512), F32)]
    pR = pA
    for qc in range(8):
        qsl = slice(512 * qc, 512 * (qc + 1))
        kc_count = 4 * (qc + 1)
        waves = [range(w, min(w + WAVE, kc_count)) for w in range(0, kc_count, WAVE)]
        for wi, kcs in enumerate(waves):
            nkc = len(kcs)
            k.dma("sp", k_wave[:, :, 0:128 * nkc],
                  k_dram.ap()[:, 128 * kcs[0]:128 * (kcs[-1] + 1)].rearrange("(c p) t -> p c t", p=128),
                  P + "kin", writes=[(P, "kwave")])
            k.dma("sp", v_wave[:, 0:nkc, :],
                  v_dram.ap()[128 * kcs[0]:128 * (kcs[-1] + 1), :].rearrange("(j p) d -> p j d", p=128),
                  P + "vin", writes=[(P, "vwave")])
            for h in range(NH):
                hb = 64 * (h % 2)
                for kc in kcs:
                    kj = kc % WAVE
                    ps, pk_ = pS[kj % 4], (P, "pS", kj % 4)
                    k.mm(ps[:, :], k_wave[:, h // 2, :][hb:hb + 64, 128 * kj:128 * (kj + 1)],
                         q_fm[:, h // 2, qsl][hb:hb + 64, :], True, True,
                         reads=[(P, "kwave"), (P, "qfm", h // 2)], writes=[pk_])
                    p_ = kc - 4 * qc
                    if p_ >= 0:
                        k.tt("dve", ps[:, :], ps[:, :], mask[:, p_, :], ALU.add,
                             reads=[pk_, (P, "mask")], writes=[pk_])
                    k.act(pt[:, kj, :], ps[:, :], AF.Exp, bias=negb[:, h:h + 1],
                          reads=[pk_, (P, "negb")], writes=[(P, "pt", kj)])
                po, pok = pO[h % 2], (P, "pO", h % 2)
                for j, kc in enumerate(kcs):
                    kj = kc % WAVE
                    k.mm(po[0:65, :], v_wave[:, kj, 65 * h:65 * (h + 1)], pt[:, kj, :],
                         j == 0, kc == kcs[-1],
                         reads=[(P, "vwave"), (P, "pt", kj)], writes=[pok])
                if wi == 0:
                    k.copy("dve", o_acc[0:65, h, :], po[0:65, :], reads=[pok], writes=[(P, "oacc", h)])
                else:
                    k.tt("dve", o_acc[0:65, h, :], o_acc[0:65, h, :], po[0:65, :], ALU.add,
                         reads=[pok, (P, "oacc", h)], writes=[(P, "oacc", h)])
        for h in range(NH):
            k.ts("dve", rn[:, :], o_acc[64:65, h, :], 1e-30, None, ALU.max,
                 reads=[(P, "oacc", h)], writes=[(P, "rn")])
            k.recip(rn[:, :], rn[:, :], reads=[(P, "rn")], writes=[(P, "rn")])
            k.mm(pR[0:64, :], OR[0:1, 0:64], rn[:, :], True, True,
                 reads=[(P, "rn"), ("ones_row_f",)], writes=[(P, "pA")])
            k.copy("dve", rb[:, :], pR[0:64, :], reads=[(P, "pA")], writes=[(P, "rb")])
            k.tt("dve", o_stg[:, h, :], o_acc[0:64, h, :], rb[:, :], ALU.mult,
                 reads=[(P, "oacc", h), (P, "rb")], writes=[(P, "ostg")])
        k.dma("sp", o_dram.ap()[:, qsl].rearrange("(h r) q -> r h q", r=64),
              o_stg[:, :, :], P + "oout", reads=[(P, "ostg")], writes=[(P, "odram", qc)])
    k.free(P + "_mask", P + "_kwave", P + "_vwave", P + "_pt", P + "_oacc", P + "_ostg",
           P + "_rn", P + "_rb", P + "_pS0", P + "_pS1", P + "_pS2", P + "_pS3",
           P + "_pO0", P + "_pO1")

    o_fm = k.sbuf(P + "_ofm", (128, NCH, S), BF16)
    for c in range(NCH):
        k.dma("sp", o_fm[:, c, :], o_dram.ap()[128 * c:128 * (c + 1), :], P + "oin",
              reads=[(P, "odram", qq) for qq in range(8)], writes=[(P, "ofm", c)])
    wot = load_wT(k, P + "_wot", io["Wo"], P + "wo")
    zt = k.sbuf(P + "_zt", (128, D), F32)
    yt = k.sbuf(P + "_yt", (128, D), F32)
    ln = LNKit(k, P + "_ln")
    for tt in range(32):
        tsl = slice(128 * tt, 128 * (tt + 1))
        k.dma("sp", zt[:, :], io["x_f32"].ap()[tsl, :], P + "xt", writes=[(P, "zt")])
        for half in range(2):
            hsl = slice(512 * half, 512 * (half + 1))
            ps, pk_ = (pA, (P, "pA")) if half == 0 else (pB, (P, "pB"))
            for c in range(NCH):
                k.mm(ps[:, :], o_fm[:, c, tsl], wot[:, c, hsl], c == 0, c == NCH - 1,
                     reads=[(P + "_wot", c), (P, "ofm", c)], writes=[pk_])
            k.tt("dve", zt[:, hsl], zt[:, hsl], ps[:, :], ALU.add,
                 reads=[(P, "zt"), pk_], writes=[(P, "zt")])
        ln.apply(zt[:, :], yt[:, :], (P, "zt"), (P, "yt"))
        k.dma("sp", io["y_f32"].ap()[tsl, :], yt[:, :], P + "y", reads=[(P, "yt")])
        k.dma("pool", io["y_bf16"].ap()[tsl, :], yt[:, :], P + "yb", reads=[(P, "yt")])
    ln.free()
    k.free(P + "_ofm", P + "_wot", P + "_zt", P + "_yt", P + "_pA", P + "_pB", P + "_qfm",
           P + "_nrm2", P + "_nmax", P + "_negb", P + "_bcol", P + "_emap", P + "_brow")



def host_consts_l3_ffn(k):
    t = np.arange(512)
    for j in range(4):
        k.const(f"c_l3f_cm{j}", (t % 4 == j).astype(np.float32).reshape(1, 512))


def emit_l3_ffn(k, io):
    P = "l3f"
    TQ = 1024
    NT = S // TQ

    wu = k.sbuf(P + "_wu", (128, NCH), BF16)
    bu = k.sbuf(P + "_bu", (1, 1), F32)
    w_tm = k.sbuf(P + "_wtm", (128, 4, 32), BF16)
    c_cols = k.sbuf(P + "_ccols", (128, 4, 32), F32)
    x_fm = load_fm(k, P + "_xfm", io["x_bf16"], P + "x")
    u512 = k.sbuf(P + "_u512", (1, 512), F32)
    cmask = [k.sbuf(P + f"_cm{j}", (1, 512), F32) for j in range(4)]
    wrow = [k.sbuf(P + f"_wr{i}", (1, 512), F32) for i in range(4)]
    crow = k.sbuf(P + "_crow", (1, 512), F32)
    tmp1 = k.sbuf(P + "_tmp1", (1, 512), F32)
    c_d = k.dram(P + "_c_d", (4, S), BF16)
    k.dma("pool", wu[:, :], io["Wu_cols"].ap(), P + "cw", writes=[(P, "wu")])
    k.dma("sp", bu[:, :], io["bu"].ap(), P + "c", writes=[(P, "bu")])
    for j in range(4):
        k.dma("sp", cmask[j][:, :], k._drams[f"c_l3f_cm{j}"].ap(), P + "c", writes=[(P, "cm", j)])
    pu = k.psum(P + "_pu", (1, 512), F32)
    for sp_ in range(8):
        ssl = slice(512 * sp_, 512 * (sp_ + 1))
        for c in range(NCH):
            k.mm(pu[0:1, :], wu[:, c:c + 1], x_fm[:, c, ssl], c == 0, c == NCH - 1,
                 reads=[(P, "wu"), (P + "_xfm", c)], writes=[(P, "pu")])
        k.act(u512[0:1, :], pu[0:1, :], AF.Sigmoid, bias=bu[0:1, 0:1],
              reads=[(P, "pu"), (P, "bu")], writes=[(P, "u512")])
        for i in range(1, 5):
            w = wrow[i - 1]
            k.ts("dve", tmp1[:, :], u512[:, :], 4.0, float(i - 1), ALU.mult, ALU.subtract,
                 reads=[(P, "u512")], writes=[(P, "tmp1")])
            k.ts("dve", tmp1[:, :], tmp1[:, :], 0.0, None, ALU.is_gt, reads=[(P, "tmp1")], writes=[(P, "tmp1")])
            k.ts("dve", w[:, :], u512[:, :], 1.0 / i, None, ALU.mult, reads=[(P, "u512")], writes=[(P, "w", i)])
            k.tt("dve", w[:, :], w[:, :], tmp1[:, :], ALU.mult,
                 reads=[(P, "w", i), (P, "tmp1")], writes=[(P, "w", i)])
        for e in range(4):
            for j in range(4):
                wsrc = wrow[(e - j) % 4]
                k.tt("dve", tmp1[:, :], cmask[j][:, :], wsrc[:, :], ALU.mult,
                     reads=[(P, "cm", j), (P, "w", ((e - j) % 4) + 1)], writes=[(P, "tmp1")])
                if j == 0:
                    k.copy("dve", crow[:, :], tmp1[:, :], reads=[(P, "tmp1")], writes=[(P, "crow")])
                else:
                    k.tt("dve", crow[:, :], crow[:, :], tmp1[:, :], ALU.add,
                         reads=[(P, "tmp1"), (P, "crow")], writes=[(P, "crow")])
            k.dma("pool", c_d.ap()[e:e + 1, ssl], crow[:, :], P + "cd", reads=[(P, "crow")],
                  writes=[(P, "cd", e)])
    for e in range(4):
        k.dma("sp", w_tm[:, e, :], c_d.ap()[e:e + 1, :].rearrange("a (b c) -> (a b) c", c=128),
              P + "ct", reads=[(P, "cd", e)], writes=[(P, "wtm")], transpose=True)
    k.copy("dve", c_cols[:, :, :], w_tm[:, :, :], reads=[(P, "wtm")], writes=[(P, "ccols")])
    k.free(P + "_u512", *[P + f"_cm{j}" for j in range(4)], *[P + f"_wr{i}" for i in range(4)],
           P + "_crow", P + "_tmp1", P + "_pu")
    b2b = k.sbuf(P + "_b2b", (128, 4, D), BF16)
    for e in range(4):
        k.dma("pool", b2b[:, e, :], io["b2_rows"].ap()[e, :, :].partition_broadcast(128),
              P + "b2", writes=[(P, "b2b")])
    b1c = k.sbuf(P + "_b1c", (128, 4, 32), F32)
    k.dma("sp", b1c[:, :, :], io["b1_cols"].ap().rearrange("e p f -> p e f"), P + "b1",
          writes=[(P, "b1c")], allow_slow_non_contiguous=True)

    w1bt = [k.dram(P + f"_w1bt{e}", (32, 128, NCH, 128), BF16) for e in range(4)]
    w2bf = [k.dram(P + f"_w2bf{e}", (4096, D), BF16) for e in range(4)]
    for e in range(4):
        for f in range(32):
            k.dma("pool", w1bt[e].ap()[f, :, :, :],
                  io["W1"].ap()[e, :, 128 * f:128 * (f + 1)].rearrange("(c p) d -> p c d", p=128),
                  P + "wc", writes=[(P, "w1bt", e)])
        k.dma("pool", w2bf[e].ap(), io["W2"].ap()[e, :, :], P + "wc", writes=[(P, "w2bf", e)])

    h_fm = k.sbuf(P + "_h", (128, 32, TQ), BF16)
    y_acc = k.sbuf(P + "_y", (128, TQ // 128, D), F32)
    w1s = [k.sbuf(P + f"_w1s{i}", (128, NCH, 128), BF16) for i in range(3)]
    w2s = [k.sbuf(P + f"_w2s{i}", (128, 512), BF16) for i in range(4)]
    ps_pool = [k.psum(P + f"_ps{i}", (128, 512), F32) for i in range(8)]
    zt = k.sbuf(P + "_zt", (128, D), F32)
    yt = k.sbuf(P + "_yt", (128, D), F32)
    ln = LNKit(k, P + "_ln")
    for tile in range(NT):
        tssl = slice(TQ * tile, TQ * (tile + 1))
        for e in range(4):
            for f in range(32):
                wbuf = w1s[f % 3]
                k.dma("act" if f % 2 == 0 else "sp", wbuf[:, :, :], w1bt[e].ap()[f, :, :, :],
                      P + "w1l" + str(f % 3), reads=[(P, "w1bt", e)], writes=[(P, "w1s", f % 3)])
                for sp_ in range(TQ // 512):
                    ps = ps_pool[sp_ % 2]
                    for c in range(NCH):
                        k.mm(ps[:, :], wbuf[:, c, :], x_fm[:, c, TQ * tile + 512 * sp_:TQ * tile + 512 * (sp_ + 1)],
                             c == 0, c == NCH - 1,
                             reads=[(P, "w1s", f % 3), (P + "_xfm", c)], writes=[(P, "ps", sp_ % 2)])
                    def fr(e2, f=f, sp_=sp_, ps=ps):
                        return e2.activation(h_fm[:, f, 512 * sp_:512 * (sp_ + 1)], ps[:, :],
                                             AF.Relu, bias=b1c[:, 0, f:f + 1])
                    def fr2(e2, f=f, sp_=sp_, ps=ps, e_=e):
                        return e2.activation(h_fm[:, f, 512 * sp_:512 * (sp_ + 1)], ps[:, :],
                                             AF.Relu, bias=b1c[:, e_, f:f + 1])
                    k.op("act", fr2, reads=[(P, "ps", sp_ % 2), (P, "b1c")], writes=[(P, "h", f)])
            for dhalf in range(2):
                dsl = slice(512 * dhalf, 512 * (dhalf + 1))
                for f in range(32):
                    wbuf = w2s[f % 4]
                    k.dma("sp" if f % 2 == 0 else "act", wbuf[:, :], w2bf[e].ap()[128 * f:128 * (f + 1), dsl],
                          P + "w2l" + str(f % 4), reads=[(P, "w2bf", e)], writes=[(P, "w2s", f % 4)])
                    for tsub in range(TQ // 128):
                        k.mm(ps_pool[tsub][:, :], h_fm[:, f, 128 * tsub:128 * (tsub + 1)], wbuf[:, :],
                             f == 0, f == 31,
                             reads=[(P, "w2s", f % 4), (P, "h", f)], writes=[(P, "ps", tsub)])
                for tsub in range(TQ // 128):
                    ccol = c_cols[:, e, (TQ // 128) * tile + tsub:(TQ // 128) * tile + tsub + 1]
                    if e == 0:
                        k.ts("dve", y_acc[:, tsub, dsl], ps_pool[tsub][:, :], ccol, None, ALU.mult,
                             reads=[(P, "ps", tsub), (P, "ccols")], writes=[(P, "y", tsub)])
                    else:
                        k.stt("dve", y_acc[:, tsub, dsl], ps_pool[tsub][:, :], ccol, y_acc[:, tsub, dsl],
                              ALU.mult, ALU.add,
                              reads=[(P, "ps", tsub), (P, "ccols"), (P, "y", tsub)], writes=[(P, "y", tsub)])
                    k.stt("dve", y_acc[:, tsub, dsl], b2b[:, e, dsl], ccol, y_acc[:, tsub, dsl],
                          ALU.mult, ALU.add,
                          reads=[(P, "b2b"), (P, "ccols"), (P, "y", tsub)], writes=[(P, "y", tsub)])
        for tsub in range(TQ // 128):
            trow = TQ * tile + 128 * tsub
            k.dma("sp", zt[:, :], io["x_f32"].ap()[trow:trow + 128, :], P + "xt", writes=[(P, "zt")])
            k.tt("dve", zt[:, :], zt[:, :], y_acc[:, tsub, :], ALU.add,
                 reads=[(P, "zt"), (P, "y", tsub)], writes=[(P, "zt")])
            ln.apply(zt[:, :], yt[:, :], (P, "zt"), (P, "yt"))
            k.dma("sp", io["y_f32"].ap()[trow:trow + 128, :], yt[:, :], P + "y2", reads=[(P, "yt")])
            k.dma("pool", io["y_bf16"].ap()[trow:trow + 128, :], yt[:, :], P + "yb", reads=[(P, "yt")])
    ln.free()
    k.free(P + "_wu", P + "_bu", P + "_xfm",
           P + "_wtm", P + "_ccols", P + "_b2b", P + "_b1c", P + "_h", P + "_y",
           P + "_w1s0", P + "_w1s1", P + "_w1s2", P + "_w2s0", P + "_w2s1",
           P + "_w2s2", P + "_w2s3",
           *[P + f"_ps{i}" for i in range(8)], P + "_zt", P + "_yt")



def host_consts_l1_attn(k):
    i_idx = np.arange(128) % 64
    inv_freq = 1.0 / 10000.0 ** (i_idx / 64.0)
    ang = np.arange(S, dtype=np.float64)[None, :] * inv_freq[:, None]
    k.const("c_l1a_cs", _bf16(np.cos(ang)))
    k.const("c_l1a_sn", _bf16(np.sin(ang)))
    Pm = np.zeros((128, 128))
    for i in range(64):
        Pm[i, i + 64] = -1.0
        Pm[i + 64, i] = 1.0
    k.const("c_l1a_rotT", _bf16(Pm.T))
    s_ = np.arange(128)[:, None]
    t_ = np.arange(128)[None, :]
    k.const("c_l1a_ltri", (s_ <= t_).astype(np.float32))
    k.const("c_l1a_trimask", _bf16(np.where(s_ <= t_, 0.0, -1e30)))


def emit_l1_attn(k, io, upto=None):
    P = "l1a"
    OC = k._sbufs["ones_col_b"]
    OR = k._sbufs["ones_row_f"]
    IDB = k._sbufs["ident_b"]
    NH = 8
    SCALE = 1.0 / math.sqrt(128.0)

    gcols = k.sbuf(P + "_gcols", (128, 32, 32), F32)
    pA = k.psum(P + "_pA", (128, 512), F32)
    pB = k.psum(P + "_pB", (128, 512), F32)
    q_dram = k.dram(P + "_q_dram", (D, S), BF16)
    k_dram = k.dram(P + "_k_dram", (D, S), BF16)
    qkstg = k.sbuf(P + "_qkstg", (128, S), BF16)
    rotT = k.sbuf(P + "_rotT", (128, 128), BF16)
    ctab = k.sbuf(P + "_ctab", (128, S), BF16)
    stab = k.sbuf(P + "_stab", (128, S), BF16)
    t0s = k.sbuf(P + "_t0s", (128, 512), BF16)
    t1s = k.sbuf(P + "_t1s", (128, 512), BF16)
    v_dram = k.dram(P + "_v_dram", (S, D), BF16)
    y_dram = k.dram(P + "_y_dram", (D, S), BF16)
    k.dma("sp", rotT[:, :], k._drams["c_l1a_rotT"].ap(), P + "c", writes=[(P, "rotT")])
    k.dma("sp", ctab[:, :], k._drams["c_l1a_cs"].ap(), P + "c", writes=[(P, "ctab")])
    k.dma("sp", stab[:, :], k._drams["c_l1a_sn"].ap(), P + "c", writes=[(P, "stab")])

    x_fm = load_fm(k, P + "_xfm", io["x_bf16"], P + "x")
    hbuf = x_fm
    nwc = k.sbuf(P + "_nwc", (128, NCH), F32)
    ssq = k.sbuf(P + "_ssq", (1, S), F32)
    eps1 = k.sbuf(P + "_eps1", (1, 1), F32)
    sqt = k.sbuf(P + "_sqt", (128, 512), BF16)
    k.dma("sp", nwc[:, :], io["norm_w_cols"].ap(), P + "c", writes=[(P, "nwc")])
    k.memset("dve", eps1[:, :], 1e-6, writes=[(P, "eps1")])
    for sp_ in range(8):
        ssl = slice(512 * sp_, 512 * (sp_ + 1))
        for c in range(NCH):
            k.act(sqt[:, :], x_fm[:, c, ssl], AF.Square, reads=[(P + "_xfm", c)], writes=[(P, "sqt")])
            k.mm(pA[0:1, :], OC[:, :], sqt[:, :], c == 0, c == NCH - 1,
                 reads=[(P, "sqt"), ("ones_col_b",)], writes=[(P, "pA")])
        k.copy("dve", ssq[0:1, ssl], pA[0:1, :], reads=[(P, "pA")], writes=[(P, "ssq")])
    k.act(ssq[:, :], ssq[:, :], AF.Ln, bias=eps1[0:1, 0:1], scale=1.0 / D,
          reads=[(P, "ssq"), (P, "eps1")], writes=[(P, "ssq")])
    k.act(ssq[:, :], ssq[:, :], AF.Exp, scale=-0.5, reads=[(P, "ssq")], writes=[(P, "ssq")])
    for sp_ in range(8):
        ssl = slice(512 * sp_, 512 * (sp_ + 1))
        k.mm(pB[:, :], OR[:, :], ssq[0:1, ssl], True, True,
             reads=[(P, "ssq"), ("ones_row_f",)], writes=[(P, "pB")])
        for c in range(NCH):
            k.stt("dve", hbuf[:, c, ssl], x_fm[:, c, ssl], nwc[:, c:c + 1], pB[:, :],
                  ALU.mult, ALU.mult, reads=[(P + "_xfm", c), (P, "nwc"), (P, "pB")],
                  writes=[(P, "h", c), (P + "_xfm", c)])
    k.free(P + "_sqt", P + "_eps1", P + "_ssq", P + "_nwc")

    g32 = k.sbuf(P + "_g32", (32, S), F32)
    bg32 = k.sbuf(P + "_bg32", (32, 1), F32)
    k.dma("sp", bg32[:, :], io["bg32"].ap(), P + "c", writes=[(P, "bg32")])
    wgt = load_wT(k, P + "_wgt", io["Wg32"], P + "wg", dout=32, din=D)
    for sp_ in range(8):
        ssl = slice(512 * sp_, 512 * (sp_ + 1))
        for c in range(NCH):
            k.mm(pA[0:32, :], wgt[:, c, :], hbuf[:, c, ssl], c == 0, c == NCH - 1,
                 reads=[(P + "_wgt", c), (P, "h", c)], writes=[(P, "pA")])
        k.act(g32[:, ssl], pA[0:32, :], AF.Sigmoid, bias=bg32[:, 0:1],
              reads=[(P, "pA"), (P, "bg32")], writes=[(P, "g32")])
    pT = k.psum(P + "_pT", (128, 128), F32)
    for c in range(32):
        def ftr(e, c=c):
            return e.transpose(pT[:, 0:32], g32[:, 128 * c:128 * (c + 1)], k._sbufs["ident_f"][0:32, 0:32])
        k.op("pe", ftr, reads=[(P, "g32"), ("ident_f",)], writes=[(P, "pT")])
        k.copy("dve", gcols[:, c, :], pT[:, 0:32], reads=[(P, "pT")], writes=[(P, "gcols")])
    k.free(P + "_wgt", P + "_g32", P + "_bg32", P + "_pT")

    if upto == "gates":
        dbg = k.dram("dbg", (128, 32, 32), F32, kind="ExternalOutput")
        k.dma("sp", dbg.ap(), gcols[:, :, :], P + "dbg", reads=[(P, "gcols")])
        return
    for (wname, wdram, dkey) in (("Wq", q_dram, "qd"), ("Wk", k_dram, "kd")):
        wtn = P + "_wt" + wname
        wt = load_wT(k, wtn, io[wname], P + "w" + wname)
        for fc in range(NCH):
            for sp_ in range(8):
                ssl = slice(512 * sp_, 512 * (sp_ + 1))
                for c in range(NCH):
                    k.mm(pA[:, :], wt[:, c, 128 * fc:128 * (fc + 1)], hbuf[:, c, ssl],
                         c == 0, c == NCH - 1, reads=[(wtn, c), (P, "h", c)], writes=[(P, "pA")])
                k.copy("dve", t0s[:, :], pA[:, :], reads=[(P, "pA")], writes=[(P, "t0s")])
                k.mm(pB[:, :], rotT[:, :], t0s[:, :], True, True,
                     reads=[(P, "rotT"), (P, "t0s")], writes=[(P, "pB")])
                k.tt("dve", qkstg[:, ssl], pB[:, :], stab[:, ssl], ALU.mult,
                     reads=[(P, "pB"), (P, "stab")], writes=[(P, "qkstg")])
                k.tt("pool", t1s[:, :], t0s[:, :], ctab[:, ssl], ALU.mult,
                     reads=[(P, "t0s"), (P, "ctab")], writes=[(P, "t1s")])
                k.tt("dve", qkstg[:, ssl], qkstg[:, ssl], t1s[:, :], ALU.add,
                     reads=[(P, "qkstg"), (P, "t1s")], writes=[(P, "qkstg")])
            k.dma("sp", wdram.ap()[128 * fc:128 * (fc + 1), :], qkstg[:, :], P + wname + "out",
                  reads=[(P, "qkstg")], writes=[(P, dkey, fc)])
        k.free(wtn)

    if upto == "qk":
        dbg = k.dram("dbg", (128, NCH, S), BF16, kind="ExternalOutput")
        for cc in range(NCH):
            k.dma("sp", dbg.ap()[:, cc, :], q_dram.ap()[128 * cc:128 * (cc + 1), :], P + "dbg",
                  reads=[(P, "qd", cc)])
        return
    wt = load_wT(k, P + "_wtv", io["Wv"], P + "wv")
    vstg = k.sbuf(P + "_vstg", (128, 4, D), BF16)
    for g_ in range(8):
        for j in range(4):
            tt_ = 4 * g_ + j
            for half in range(2):
                ps, pk_ = (pA, (P, "pA")) if half == 0 else (pB, (P, "pB"))
                for c in range(NCH):
                    k.mm(ps[:, :], hbuf[:, c, 128 * tt_:128 * (tt_ + 1)], wt[:, c, 512 * half:512 * (half + 1)],
                         c == 0, c == NCH - 1, reads=[(P + "_wtv", c), (P, "h", c)], writes=[pk_])
                k.copy("dve", vstg[:, j, 512 * half:512 * (half + 1)], ps[:, :],
                       reads=[pk_], writes=[(P, "vstg", j)])
        k.dma("sp", v_dram.ap()[512 * g_:512 * (g_ + 1), :].rearrange("(j p) d -> p j d", p=128),
              vstg[:, :, :], P + "vout", reads=[(P, "vstg", jj) for jj in range(4)],
              writes=[(P, "vdram", g_)])
    k.free(P + "_wtv", P + "_vstg", P + "_xfm")

    ltri = k.sbuf(P + "_ltri", (128, 128), F32)
    trim = k.sbuf(P + "_trim", (128, 128), BF16)
    k.dma("sp", ltri[:, :], k._drams["c_l1a_ltri"].ap(), P + "c2", writes=[(P, "ltri")])
    k.dma("sp", trim[:, :], k._drams["c_l1a_trimask"].ap(), P + "c2", writes=[(P, "trim")])
    M32 = k.sbuf(P + "_M", (128, NH, 128), F32)
    Mb = k.sbuf(P + "_Mb", (128, NH, 128), BF16)
    vbuf = [k.sbuf(P + f"_vb{i}", (128, D), BF16) for i in range(2)]
    qbuf = [k.sbuf(P + f"_qb{i}", (128, NH, 128), BF16) for i in range(2)]
    kbuf = [k.sbuf(P + f"_kb{i}", (128, NH, 128), BF16) for i in range(2)]
    lnf = k.sbuf(P + "_lnf", (128, NH), F32)
    clf = k.sbuf(P + "_clf", (128, NH), F32)
    ecol = k.sbuf(P + "_ecol", (128, NH), F32)
    gsc = k.sbuf(P + "_gsc", (128, NH), F32)
    wkc = k.sbuf(P + "_wkc", (128, 1), F32)
    crow = k.sbuf(P + "_crow", (1, NH), F32)
    dtot = k.sbuf(P + "_dtot", (1, NH), F32)
    dcolb = k.sbuf(P + "_dcolb", (128, NH), F32)
    clfb = k.sbuf(P + "_clfb", (128, 2 * NH), F32)
    rows = k.sbuf(P + "_rows", (1, 2 * NH, 128), F32)
    drow = k.sbuf(P + "_drow", (1, 128), F32)
    gam = k.sbuf(P + "_gam", (128, 128), BF16)
    abuf = k.sbuf(P + "_abuf", (128, 128), BF16)
    qtl = k.sbuf(P + "_qtl", (128, 128), BF16)
    ktl = k.sbuf(P + "_ktl", (128, 128), BF16)
    ogb = k.sbuf(P + "_ogb", (128, 128), BF16)
    ystg = k.sbuf(P + "_ystg", (128, NH, 128), BF16)
    pQK = k.psum(P + "_pQK", (128, 128), F32)
    pG = k.psum(P + "_pG", (128, 128), F32)
    pY = k.psum(P + "_pY", (128, 128), F32)
    pM = k.psum(P + "_pM", (128, 128), F32)
    pKT = k.psum(P + "_pKT", (128, 128), BF16)
    k.memset("dve", M32[:, :, :], 0.0, writes=[(P, "M")])
    k.memset("dve", Mb[:, :, :], 0.0, writes=[(P, "Mb")])
    OCF = k._sbufs["ones_col_f"]
    import os as _o1
    _cmax = int(_o1.environ.get("L1A_CMAX", "32"))
    _parts = set((_o1.environ.get("L1A_PARTS") or "all").split(","))
    for c in range(_cmax):
        csl = slice(128 * c, 128 * (c + 1))
        k.dma("sp", vbuf[c % 2][:, :], v_dram.ap()[csl, :], P + "vin" + str(c % 2),
              reads=[(P, "vdram", c // 4)], writes=[(P, "vb", c % 2)])
        k.dma("sp", qbuf[c % 2][:, :, :], q_dram.ap()[:, csl].rearrange("(a p) t -> p a t", p=128),
              P + "qin" + str(c % 2), reads=[(P, "qd", ch) for ch in range(NCH)],
              writes=[(P, "qb", c % 2)])
        k.dma("sp", kbuf[c % 2][:, :, :], k_dram.ap()[:, csl].rearrange("(a p) t -> p a t", p=128),
              P + "kin" + str(c % 2), reads=[(P, "kd", ch) for ch in range(NCH)],
              writes=[(P, "kb", c % 2)])
        if "pre" not in _parts and "all" not in _parts:
            continue
        k.ts("dve", lnf[:, :], gcols[:, c, 0:8], 1e-30, None, ALU.max,
             reads=[(P, "gcols")], writes=[(P, "lnf")])
        k.act(lnf[:, :], lnf[:, :], AF.Ln, reads=[(P, "lnf")], writes=[(P, "lnf")])
        k.ts("dve", lnf[:, :], lnf[:, :], -60.0, None, ALU.max, reads=[(P, "lnf")], writes=[(P, "lnf")])
        k.ts("dve", gsc[:, :], gcols[:, c, 8:16], SCALE, None, ALU.mult,
             reads=[(P, "gcols")], writes=[(P, "gsc")])
        k.mm(pQK[:, 0:8], ltri[:, :], lnf[:, :], True, True,
             reads=[(P, "ltri"), (P, "lnf")], writes=[(P, "pQK")])
        k.copy("dve", clf[:, :], pQK[:, 0:8], reads=[(P, "pQK")], writes=[(P, "clf")])
        if "rows" not in _parts and "all" not in _parts:
            continue
        k.copy("dve", clfb[:, 0:NH], clf[:, :], reads=[(P, "clf")], writes=[(P, "clfb")])
        k.copy("dve", clfb[:, NH:2 * NH], gcols[:, c, 16:24], reads=[(P, "gcols")], writes=[(P, "clfb")])
        for j in range(2 * NH):
            pr, prk = (pG, (P, "pG")) if j % 2 == 0 else (pQK, (P, "pQK"))
            def ftr2(e, j=j, pr=pr):
                return e.transpose(pr[0:1, 0:128], clfb[:, j:j + 1], k._sbufs["ident_f"][:, :])
            k.op("pe", ftr2, reads=[(P, "clfb"), ("ident_f",)], writes=[prk])
            k.copy("dve", rows[:, j, :], pr[0:1, 0:128], reads=[prk], writes=[(P, "rows")])
        if "bc" not in _parts and "all" not in _parts:
            continue
        k.copy("dve", bcast_free(crow[:, :], 1), rows[0:1, 0:NH, 127:128],
               reads=[(P, "rows")], writes=[(P, "crow")])
        k.act(dtot[:, :], crow[:, :], AF.Exp, reads=[(P, "crow")], writes=[(P, "dtot")])
        k.mm(pG[:, 0:8], OR[:, :], dtot[0:1, :], True, True,
             reads=[(P, "dtot"), ("ones_row_f",)], writes=[(P, "pG")])
        k.copy("dve", dcolb[:, :], pG[:, 0:8], reads=[(P, "pG")], writes=[(P, "dcolb")])
        k.mm(pG[:, 0:8], OR[:, :], crow[0:1, :], True, True,
             reads=[(P, "crow"), ("ones_row_f",)], writes=[(P, "pG")])
        k.tt("dve", ecol[:, :], pG[:, 0:8], clf[:, :], ALU.subtract,
             reads=[(P, "pG"), (P, "clf")], writes=[(P, "ecol")])
        k.act(ecol[:, :], ecol[:, :], AF.Exp, reads=[(P, "ecol")], writes=[(P, "ecol")])
        for h in range(NH if "all" in _parts or "h" in _parts else 0):
            k.mm(pG[:, :], OR[:, :], rows[0:1, h, :], True, True,
                 reads=[(P, "rows"), ("ones_row_f",)], writes=[(P, "pG")])
            k.ts("dve", pG[:, :], pG[:, :], clf[:, h:h + 1], None, ALU.subtract,
                 reads=[(P, "pG"), (P, "clf")], writes=[(P, "pG")])
            k.tt("dve", pG[:, :], pG[:, :], trim[:, :], ALU.add,
                 reads=[(P, "pG"), (P, "trim")], writes=[(P, "pG")])
            k.act(gam[:, :], pG[:, :], AF.Exp, reads=[(P, "pG")], writes=[(P, "gam")])
            k.mm(pQK[:, :], kbuf[c % 2][:, h, :], qbuf[c % 2][:, h, :], True, True,
                 reads=[(P, "kb", c % 2), (P, "qb", c % 2)], writes=[(P, "pQK")])
            k.tt("dve", abuf[:, :], pQK[:, :], gam[:, :], ALU.mult,
                 reads=[(P, "pQK"), (P, "gam")], writes=[(P, "abuf")])
            k.ts("dve", abuf[:, :], abuf[:, :], gsc[:, h:h + 1], None, ALU.mult,
                 reads=[(P, "abuf"), (P, "gsc")], writes=[(P, "abuf")])
            k.act(drow[:, :], rows[0:1, h, :], AF.Exp, reads=[(P, "rows")], writes=[(P, "drow")])
            k.mm(pG[:, :], OR[:, :], drow[0:1, :], True, True,
                 reads=[(P, "drow"), ("ones_row_f",)], writes=[(P, "pG")])
            k.tt("dve", qtl[:, :], pG[:, :], qbuf[c % 2][:, h, :], ALU.mult,
                 reads=[(P, "pG"), (P, "qb", c % 2)], writes=[(P, "qtl")])
            k.mm(pY[:, :], vbuf[c % 2][:, 128 * h:128 * (h + 1)], abuf[:, :], True, False,
                 reads=[(P, "vb", c % 2), (P, "abuf")], writes=[(P, "pY")])
            k.mm(pY[:, :], Mb[:, h, :], qtl[:, :], False, True,
                 reads=[(P, "Mb"), (P, "qtl")], writes=[(P, "pY")])
            k.mm(pG[:, :], OR[:, :], rows[0:1, NH + h, :], True, True,
                 reads=[(P, "rows"), ("ones_row_f",)], writes=[(P, "pG")])
            k.copy("dve", ogb[:, :], pG[:, :], reads=[(P, "pG")], writes=[(P, "ogb")])
            k.tt("dve", ystg[:, h, :], pY[:, :], ogb[:, :], ALU.mult,
                 reads=[(P, "pY"), (P, "ogb")], writes=[(P, "ystg", h)])
            def ftr3(e, h=h, c=c):
                return e.transpose(pKT[:, :], kbuf[c % 2][:, h, :], IDB[:, :])
            k.op("pe", ftr3, reads=[(P, "kb", c % 2), ("ident_b",)], writes=[(P, "pKT")])
            k.tt("dve", wkc[:, 0:1], ecol[:, h:h + 1], gsc[:, h:h + 1], ALU.mult,
                 reads=[(P, "ecol"), (P, "gsc")], writes=[(P, "wkc")])
            k.ts("dve", ktl[:, :], pKT[:, :], wkc[:, 0:1], None, ALU.mult,
                 reads=[(P, "pKT"), (P, "wkc")], writes=[(P, "ktl")])
            k.mm(pM[:, :], ktl[:, :], vbuf[c % 2][:, 128 * h:128 * (h + 1)],
                 True, True, reads=[(P, "ktl"), (P, "vb", c % 2)], writes=[(P, "pM")])
            k.ts("dve", M32[:, h, :], M32[:, h, :], dcolb[:, h:h + 1], None, ALU.mult,
                 reads=[(P, "M"), (P, "dcolb")], writes=[(P, "M")])
            k.tt("dve", M32[:, h, :], M32[:, h, :], pM[:, :], ALU.add,
                 reads=[(P, "M"), (P, "pM")], writes=[(P, "M")])
            k.copy("dve", Mb[:, h, :], M32[:, h, :], reads=[(P, "M")], writes=[(P, "Mb")])
        k.dma("sp", y_dram.ap()[:, csl].rearrange("(h p) t -> p h t", p=128), ystg[:, :, :],
              P + "yout", reads=[(P, "ystg", h) for h in range(NH)], writes=[(P, "ydram", c)])
    k.free(P + "_ltri", P + "_trim", P + "_M", P + "_Mb",
           P + "_vb0", P + "_vb1", P + "_qb0", P + "_qb1", P + "_kb0", P + "_kb1",
           P + "_lnf", P + "_clf", P + "_ecol", P + "_gsc",
           P + "_wkc", P + "_crow", P + "_dtot", P + "_dcolb", P + "_clfb", P + "_rows", P + "_drow",
           P + "_gam", P + "_abuf", P + "_qtl", P + "_ktl", P + "_ogb", P + "_ystg",
           P + "_pQK", P + "_pG", P + "_pY", P + "_pM", P + "_pKT",
           P + "_rotT", P + "_ctab", P + "_stab", P + "_t0s", P + "_t1s", P + "_qkstg")

    if upto == "scan":
        dbg = k.dram("dbg", (128, NCH, S), BF16, kind="ExternalOutput")
        for cc in range(NCH):
            k.dma("sp", dbg.ap()[:, cc, :], y_dram.ap()[128 * cc:128 * (cc + 1), :], P + "dbg",
                  reads=[(P, "ydram", ch) for ch in range(32)])
        return
    y_fm = k.sbuf(P + "_yfm", (128, NCH, S), BF16)
    for cc in range(NCH):
        k.dma("sp", y_fm[:, cc, :], y_dram.ap()[128 * cc:128 * (cc + 1), :], P + "yin",
              reads=[(P, "ydram", ch) for ch in range(32)], writes=[(P, "yfm", cc)])
    wot = load_wT(k, P + "_wot", io["Wout"], P + "wo")
    bob = k.sbuf(P + "_bob", (128, D), F32)
    for hh in range(2):
        k.dma("sp", bob[0:1, 512 * hh:512 * (hh + 1)], io["bout_row"].ap()[0:1, 512 * hh:512 * (hh + 1)],
              P + "c3", writes=[(P, "bobr")])
        k.mm(pA[:, :], OR[:, :], bob[0:1, 512 * hh:512 * (hh + 1)], True, True,
             reads=[(P, "bobr"), ("ones_row_f",)], writes=[(P, "pA")])
        k.copy("dve", bob[:, 512 * hh:512 * (hh + 1)], pA[:, :], reads=[(P, "pA")], writes=[(P, "bobr")])
    zt = k.sbuf(P + "_zt", (128, D), F32)
    yt = k.sbuf(P + "_yt", (128, D), F32)
    ln = LNKit(k, P + "_ln")
    for tt_ in range(32):
        tsl = slice(128 * tt_, 128 * (tt_ + 1))
        k.dma("sp", zt[:, :], io["x_f32"].ap()[tsl, :], P + "xt", writes=[(P, "zt")])
        for half in range(2):
            hsl = slice(512 * half, 512 * (half + 1))
            ps, pk_ = (pA, (P, "pA")) if half == 0 else (pB, (P, "pB"))
            for cc in range(NCH):
                k.mm(ps[:, :], y_fm[:, cc, tsl], wot[:, cc, hsl], cc == 0, cc == NCH - 1,
                     reads=[(P + "_wot", cc), (P, "yfm", cc)], writes=[pk_])
            k.tt("dve", zt[:, hsl], zt[:, hsl], ps[:, :], ALU.add,
                 reads=[(P, "zt"), pk_], writes=[(P, "zt")])
        k.tt("dve", zt[:, :], zt[:, :], bob[:, :], ALU.add,
             reads=[(P, "zt"), (P, "bobr")], writes=[(P, "zt")])
        ln.apply(zt[:, :], yt[:, :], (P, "zt"), (P, "yt"))
        k.dma("sp", io["y_f32"].ap()[tsl, :], yt[:, :], P + "y", reads=[(P, "yt")])
        k.dma("pool", io["y_bf16"].ap()[tsl, :], yt[:, :], P + "yb", reads=[(P, "yt")])
    ln.free()
    k.free(P + "_yfm", P + "_wot", P + "_bob", P + "_zt", P + "_yt", P + "_pA", P + "_pB",
           P + "_gcols")



def host_consts_l0_ffn(k):
    HEADS, DH, WIN = 8, 64, 512
    j = np.arange(128) % 64
    i_idx = (j % 32).astype(np.float64)
    inv_freq = 1.0 / 10000.0 ** (2.0 * i_idx / DH)
    t = np.arange(S, dtype=np.float64)
    ang = t[None, :] * inv_freq[:, None]
    base = (2.0 * i_idx + 0.4 * DH) / (1.4 * DH)
    power = (t - S // 2) / float(WIN // 2)
    scl = base[:, None] ** power[None, :]
    cos, sin = np.cos(ang), np.sin(ang)
    QKS = 8.0
    k.const("c_l0f_cq", _bf16(cos * scl * QKS))
    k.const("c_l0f_sq", _bf16(sin * scl * QKS))
    k.const("c_l0f_ck", _bf16(cos / scl))
    k.const("c_l0f_sk", _bf16(sin / scl))
    Pm = np.zeros((128, 128))
    for p in range(128):
        jj = p % 64
        b0 = p - jj
        if jj < 32:
            Pm[p, b0 + jj + 32] = -1.0
        else:
            Pm[p, b0 + jj - 32] = 1.0
    k.const("c_l0f_rotT", _bf16(Pm.T))
    kj = np.arange(128)[:, None]
    qi = np.arange(512)[None, :]
    m = np.zeros((128, 8, 512), np.float64)
    for kc in range(8):
        r = qi - kj + 512 - 128 * kc
        m[:, kc, :] = np.where((r >= 0) & (r <= 512), 0.0, -1e30)
    k.const("c_l0f_mask", _bf16(m))
    Eq = np.zeros((128, 4, 8))
    E8T = np.zeros((8, 4, 128))
    for c in range(4):
        Eq[0:64, c, 2 * c] = 1.0
        Eq[64:128, c, 2 * c + 1] = 1.0
        E8T[2 * c, c, 0:64] = 1.0
        E8T[2 * c + 1, c, 64:128] = 1.0
    k.const("c_l0f_Eq", _bf16(Eq))
    k.const("c_l0f_E8T", _bf16(E8T))


def emit_l0_ffn_layer(k, io, l, part1_only=False):
    P = f"l0f{l}"
    OR = k._sbufs["ones_row_f"]
    OC = k._sbufs["ones_col_b"]
    IDB = k._sbufs["ident_b"]
    HEADS = 8

    qkv = k.sbuf(P + "_qkv", (128, 12, S), BF16)
    o_dram = k.dram(P + "_o_dram", (512, S), BF16)

    pS = [k.psum(P + "_pS0", (128, 512), F32), k.psum(P + "_pS1", (128, 512), F32)]
    pO = [k.psum(P + "_pO0", (128, 512), F32), k.psum(P + "_pO1", (128, 512), F32)]
    pR = k.psum(P + "_pR", (128, 512), F32)
    pN = k.psum(P + "_pN", (8, 512), F32)
    x_fm = k.sbuf(P + "_xfm", (128, NCH, S), BF16)
    wq = k.sbuf(P + "_wq", (128, NCH, 1536), BF16)
    pA = k.psum(P + "_pA", (128, 512), F32)
    pB = k.psum(P + "_pB", (128, 512), F32)
    for c in range(4):
        k.dma("sp", x_fm[:, c, :], io["xi_bf16"].ap()[:, 128 * c:128 * (c + 1)], P + "x",
              writes=[(P + "_xfm", c)], transpose=True)
    for c in range(4, NCH):
        k.dma("sp", x_fm[:, c, :], io["xi_sh"].ap()[:, 128 * (c - 4):128 * (c - 3)], P + "x",
              writes=[(P + "_xfm", c)], transpose=True)
    k.dma("pool", wq[:, :, :], io[f"wqkv{l}"].ap().rearrange("(c p) d -> p c d", p=128),
          P + "wq", writes=[(P, "wq")])
    for oc in range(12):
        for sp_ in range(8):
            ssl = slice(512 * sp_, 512 * (sp_ + 1))
            ps, pk_ = (pA, (P, "pA")) if sp_ % 2 == 0 else (pB, (P, "pB"))
            for c in range(NCH):
                k.mm(ps[:, :], wq[:, c, 128 * oc:128 * (oc + 1)], x_fm[:, c, ssl],
                     c == 0, c == NCH - 1, reads=[(P, "wq"), (P + "_xfm", c)], writes=[pk_])
            k.copy("dve", qkv[:, oc, ssl], ps[:, :], reads=[pk_], writes=[(P, "qkv", oc)])
    k.free(P + "_xfm", P + "_wq")

    v_tm = k.sbuf(P + "_vtm", (128, 32, 8, 66), BF16)
    o_stg = k.sbuf(P + "_ostg", (64, 8, 512), BF16)
    masks = k.sbuf(P + "_mask", (128, 8, 512), BF16)
    pt = k.sbuf(P + "_pt", (128, 8, 512), BF16)
    nrow = k.sbuf(P + "_nrow", (8, S), BF16)
    nrow2 = k.sbuf(P + "_nrow2", (8, S), BF16)
    eqm = k.sbuf(P + "_eqm", (128, 4, 8), BF16)
    e8t = k.sbuf(P + "_e8t", (8, 4, 128), BF16)
    qsc = k.sbuf(P + "_qsc", (128, 2), F32)
    sstg = k.sbuf(P + "_sstg", (128, 512), BF16)
    bcol = k.sbuf(P + "_bcol", (8, 4), F32)
    brow = k.sbuf(P + "_brow", (1, 8), F32)
    negb = k.sbuf(P + "_negb", (128, 8), F32)
    rn = k.sbuf(P + "_rn", (1, 512), F32)
    rb = k.sbuf(P + "_rb", (64, 512), F32)
    rotT = k.sbuf(P + "_rotT", (128, 128), BF16)
    ctab = k.sbuf(P + "_ctab", (128, S), BF16)
    stab = k.sbuf(P + "_stab", (128, S), BF16)
    t1s = k.sbuf(P + "_t1s", (128, 512), BF16)
    brow_d = k.dram(P + "_brow_d", (8, 1), F32)
    k.dma("sp", masks[:, :, :], k._drams["c_l0f_mask"].ap(), P + "c", writes=[(P, "mask")])
    k.dma("sp", eqm[:, :, :], k._drams["c_l0f_Eq"].ap(), P + "c", writes=[(P, "eqm")])
    k.dma("sp", e8t[:, :, :], k._drams["c_l0f_E8T"].ap(), P + "c", writes=[(P, "e8t")])
    k.dma("sp", qsc[:, 0:1], io[f"qs_col{l}"].ap(), P + "c", writes=[(P, "qsc")])
    k.dma("sp", qsc[:, 1:2], io[f"ks_col{l}"].ap(), P + "c", writes=[(P, "qsc")])
    k.dma("sp", rotT[:, :], k._drams["c_l0f_rotT"].ap(), P + "c2", writes=[(P, "rotT")])
    k.memset("dve", v_tm[:, :, :, 64:65], 1.0, writes=[(P, "vones")])

    for part in range(2):
        for sp_ in range(8):
            ssl = slice(512 * sp_, 512 * (sp_ + 1))
            for c in range(4):
                k.act(sstg[:, :], qkv[:, 4 * part + c, ssl], AF.Square,
                      reads=[(P, "qkv", 4 * part + c)], writes=[(P, "sstg")])
                k.mm(pN[0:8, :], eqm[:, c, :], sstg[:, :], c == 0, c == 3,
                     reads=[(P, "sstg"), (P, "eqm")], writes=[(P, "pN")])
            k.copy("dve", nrow[:, ssl], pN[0:8, :], reads=[(P, "pN")], writes=[(P, "nrow")])
        k.act(nrow[:, :], nrow[:, :], AF.Ln, reads=[(P, "nrow")], writes=[(P, "nrow")])
        k.act(nrow[:, :], nrow[:, :], AF.Exp, scale=-0.5, reads=[(P, "nrow")], writes=[(P, "nrow")])
        k.ts("dve", nrow[:, :], nrow[:, :], 1e12, None, ALU.min, reads=[(P, "nrow")], writes=[(P, "nrow")])
        for sp_ in range(8):
            ssl = slice(512 * sp_, 512 * (sp_ + 1))
            for c in range(4):
                k.mm(pR[:, :], e8t[:, c, :], nrow[0:8, ssl], True, True,
                     reads=[(P, "nrow"), (P, "e8t")], writes=[(P, "pR")])
                k.stt("dve", qkv[:, 4 * part + c, ssl], qkv[:, 4 * part + c, ssl],
                      qsc[:, part:part + 1], pR[:, :], ALU.mult, ALU.mult,
                      reads=[(P, "qkv", 4 * part + c), (P, "qsc"), (P, "pR")],
                      writes=[(P, "qkv", 4 * part + c)])
    import os as _oo
    if _oo.environ.get("L0F_DUMPQ") == "qn" and l == 0:
        dbgq = k.dram("dbgq", (128, 12, S), BF16, kind="ExternalOutput")
        k.barrier()
        k.dma("sp", dbgq.ap(), qkv[:, :, :], P + "dq")
    for part, dstrow, dkey in ((0, nrow, "nrow"), (1, nrow2, "nrow2")):
        for sp_ in range(8):
            ssl = slice(512 * sp_, 512 * (sp_ + 1))
            for c in range(4):
                k.act(sstg[:, :], qkv[:, 4 * part + c, ssl], AF.Square,
                      reads=[(P, "qkv", 4 * part + c)], writes=[(P, "sstg")])
                k.mm(pN[0:8, :], eqm[:, c, :], sstg[:, :], c == 0, c == 3,
                     reads=[(P, "sstg"), (P, "eqm")], writes=[(P, "pN")])
            k.copy("dve", dstrow[:, ssl], pN[0:8, :], reads=[(P, "pN")], writes=[(P, dkey)])
    for part in range(2):
        k.dma("sp", ctab[:, :], k._drams["c_l0f_cq" if part == 0 else "c_l0f_ck"].ap(),
              P + "c3", writes=[(P, "ctab")])
        k.dma("sp", stab[:, :], k._drams["c_l0f_sq" if part == 0 else "c_l0f_sk"].ap(),
              P + "c3", writes=[(P, "stab")])
        for c in range(4):
            oc = 4 * part + c
            for sp_ in range(8):
                ssl = slice(512 * sp_, 512 * (sp_ + 1))
                ps, pk_ = (pA, (P, "pA")) if sp_ % 2 == 0 else (pB, (P, "pB"))
                k.mm(ps[:, :], rotT[:, :], qkv[:, oc, ssl], True, True,
                     reads=[(P, "rotT"), (P, "qkv", oc)], writes=[pk_])
                k.tt("pool", t1s[:, :], qkv[:, oc, ssl], ctab[:, ssl], ALU.mult,
                     reads=[(P, "qkv", oc), (P, "ctab")], writes=[(P, "t1s")])
                k.tt("dve", qkv[:, oc, ssl], ps[:, :], stab[:, ssl], ALU.mult,
                     reads=[pk_, (P, "stab")], writes=[(P, "qkv", oc)])
                k.tt("dve", qkv[:, oc, ssl], qkv[:, oc, ssl], t1s[:, :], ALU.add,
                     reads=[(P, "qkv", oc), (P, "t1s")], writes=[(P, "qkv", oc)])
    if _oo.environ.get("L0F_DUMPQ") == "qr" and l == 0:
        dbgq = k.dram("dbgq", (128, 12, S), BF16, kind="ExternalOutput")
        k.barrier()
        k.dma("sp", dbgq.ap(), qkv[:, :, :], P + "dq")
    k.free(P + "_pA", P + "_pB")
    pVT = [k.psum(P + "_pVT0", (128, 128), BF16), k.psum(P + "_pVT1", (128, 128), BF16)]
    for c in range(4):
        for tt_ in range(32):
            def ftv(e, c=c, tt_=tt_):
                return e.transpose(pVT[tt_ % 2][:, :], qkv[:, 8 + c, 128 * tt_:128 * (tt_ + 1)], IDB[:, :])
            k.op("pe", ftv, reads=[(P, "qkv", 8 + c), ("ident_b",)], writes=[(P, "pVT", tt_ % 2)])
            k.copy("dve", v_tm[:, tt_, 2 * c:2 * c + 2, 0:64],
                   pVT[tt_ % 2][:, :].rearrange("p (a b) -> p a b", a=2),
                   reads=[(P, "pVT", tt_ % 2)], writes=[(P, "vtm")])

    for wb in range(8):
        qsl = slice(512 * wb, 512 * (wb + 1))
        kc_lo = 4 if wb == 0 else 0
        k.red(bcol[:, 0:1], nrow[:, qsl], AX.X, ALU.max, reads=[(P, "nrow")], writes=[(P, "bcol")])
        ksl_lo = 512 * (wb - 1) if wb > 0 else 0
        k.red(bcol[:, 1:2], nrow2[:, ksl_lo:512 * (wb + 1)], AX.X, ALU.max,
              reads=[(P, "nrow2")], writes=[(P, "bcol")])
        k.tt("dve", bcol[:, 0:1], bcol[:, 0:1], bcol[:, 1:2], ALU.mult,
             reads=[(P, "bcol")], writes=[(P, "bcol")])
        k.act(bcol[:, 0:1], bcol[:, 0:1], AF.Ln, reads=[(P, "bcol")], writes=[(P, "bcol")])
        k.act(bcol[:, 0:1], bcol[:, 0:1], AF.Exp, scale=0.5, reads=[(P, "bcol")], writes=[(P, "bcol")])
        k.ts("dve", bcol[:, 2:3], bcol[:, 0:1], -8.0, None, ALU.mult, reads=[(P, "bcol")], writes=[(P, "bcol")])
        k.dma("sp", brow_d.ap(), bcol[:, 2:3], P + "br", reads=[(P, "bcol")], writes=[(P, "browd")])
        k.dma("sp", brow[:, :], brow_d.ap().rearrange("a b -> b a"), P + "br",
              reads=[(P, "browd")], writes=[(P, "brow")])
        k.mm(pR[:, 0:8], OR[:, :], brow[:, :], True, True,
             reads=[(P, "brow"), ("ones_row_f",)], writes=[(P, "pR")])
        k.copy("dve", negb[:, :], pR[:, 0:8], reads=[(P, "pR")], writes=[(P, "negb")])
        for h in range(HEADS):
            hb = 64 * (h % 2)
            qx = qkv[:, h // 2, qsl][hb:hb + 64, :]
            for kc in range(kc_lo, 8):
                kpos0 = 512 * (wb - 1) + 128 * kc
                ps, pk_ = pS[h % 2], (P, "pS", h % 2)
                k.mm(ps[:, :], qkv[:, 4 + h // 2, kpos0:kpos0 + 128][hb:hb + 64, :], qx,
                     True, True, reads=[(P, "qkv", 4 + h // 2), (P, "qkv", h // 2)], writes=[pk_])
                k.tt("dve", ps[:, :], ps[:, :], masks[:, kc, :], ALU.add,
                     reads=[pk_, (P, "mask")], writes=[pk_])
                k.act(pt[:, kc, :], ps[:, :], AF.Exp, bias=negb[:, h:h + 1],
                      reads=[pk_, (P, "negb")], writes=[(P, "pt", kc)])
            po, pok = pO[h % 2], (P, "pO", h % 2)
            for j, kc in enumerate(range(kc_lo, 8)):
                ktile = 4 * (wb - 1) + kc
                k.mm(po[0:65, :], v_tm[:, ktile, h, 0:65], pt[:, kc, :],
                     j == 0, kc == 7, reads=[(P, "vtm"), (P, "pt", kc), (P, "vones")], writes=[pok])
            k.ts("dve", rn[:, :], po[64:65, :], 1e-30, None, ALU.max, reads=[pok], writes=[(P, "rn")])
            k.recip(rn[:, :], rn[:, :], reads=[(P, "rn")], writes=[(P, "rn")])
            k.mm(pR[0:64, :], OR[0:1, 0:64], rn[:, :], True, True,
                 reads=[(P, "rn"), ("ones_row_f",)], writes=[(P, "pR")])
            k.copy("dve", rb[:, :], pR[0:64, :], reads=[(P, "pR")], writes=[(P, "rb")])
            k.tt("dve", o_stg[:, h, :], po[0:64, :], rb[:, :], ALU.mult,
                 reads=[pok, (P, "rb")], writes=[(P, "ostg")])
        k.dma("sp", o_dram.ap()[:, qsl].rearrange("(h r) q -> r h q", r=64), o_stg[:, :, :],
              P + "oout", reads=[(P, "ostg")], writes=[(P, "odram", wb)])
    k.free(P + "_vtm", P + "_ostg", P + "_mask", P + "_pt", P + "_nrow", P + "_nrow2",
           P + "_eqm", P + "_e8t", P + "_qsc", P + "_sstg", P + "_bcol", P + "_brow",
           P + "_negb", P + "_rn", P + "_rb", P + "_rotT", P + "_ctab", P + "_stab",
           P + "_t1s", P + "_pVT0", P + "_pVT1", P + "_qkv")

    pA = k.psum(P + "_pAw", (128, 512), F32)
    pB = k.psum(P + "_pBw", (128, 512), F32)
    o_fm = k.sbuf(P + "_ofm2", (128, 4, S), BF16)
    wo = k.sbuf(P + "_wo", (128, 4, D), BF16)
    xt = k.sbuf(P + "_xt", (128, D), F32)
    zsh = k.sbuf(P + "_zsh", (128, 512), BF16)
    for c in range(4):
        k.dma("sp", o_fm[:, c, :], o_dram.ap()[128 * c:128 * (c + 1), :], P + "oin",
              reads=[(P, "odram", wb2) for wb2 in range(8)], writes=[(P, "ofm", c)])
    k.dma("pool", wo[:, :, :], io[f"wo{l}"].ap().rearrange("(c p) d -> p c d", p=128),
          P + "wo", writes=[(P, "wo")])
    for tt_ in range(32):
        tsl = slice(128 * tt_, 128 * (tt_ + 1))
        k.dma("sp", xt[:, :], io["xi_f32"].ap()[tsl, :], P + "xt", writes=[(P, "xt")])
        for half in range(2):
            hsl = slice(512 * half, 512 * (half + 1))
            ps, pk_ = (pA, (P, "pA")) if half == 0 else (pB, (P, "pB"))
            for c in range(4):
                k.mm(ps[:, :], o_fm[:, c, tsl], wo[:, c, hsl], c == 0, c == 3,
                     reads=[(P, "wo"), (P, "ofm", c)], writes=[pk_])
            k.tt("dve", xt[:, hsl], xt[:, hsl], ps[:, :], ALU.add,
                 reads=[(P, "xt"), pk_], writes=[(P, "xt")])
        k.dma("sp", io["xi_f32"].ap()[tsl, :], xt[:, :], P + "xo", reads=[(P, "xt")])
        k.dma("pool", io["xi_bf16"].ap()[tsl, :], xt[:, :], P + "xob", reads=[(P, "xt")])
        k.copy("dve", zsh[:, :], xt[:, 512:1024], reads=[(P, "xt")], writes=[(P, "zsh")])
        lo = 128 * tt_ + 1
        hi = min(128 * (tt_ + 1) + 1, S)
        k.dma("pool", io["xi_sh"].ap()[lo:hi, :], zsh[0:hi - lo, :], P + "xos", reads=[(P, "zsh")])
    k.free(P + "_ofm2", P + "_wo", P + "_xt", P + "_zsh", P + "_pAw", P + "_pBw",
           P + "_pS0", P + "_pS1", P + "_pO0", P + "_pO1", P + "_pR", P + "_pN")
    if part1_only:
        return

    P2 = P + "f"
    OC44 = [128] * 42 + [42, 42]
    h2n = k.sbuf(P2 + "_h2n", (128, NCH, S), BF16)
    w2 = k.sbuf(P2 + "_w2", (128, 22, D), BF16)
    astg = k.sbuf(P2 + "_astg", (128, 512), BF16)
    b1c = k.sbuf(P2 + "_b1c", (128, 44), F32)
    lgc = k.sbuf(P2 + "_lgc", (128, 2, NCH), F32)
    b2b = k.sbuf(P2 + "_b2b", (128, D), BF16)
    mrow = k.sbuf(P2 + "_mrow", (1, 512), F32)
    rrow = k.sbuf(P2 + "_rrow", (1, 512), F32)
    sstg2 = k.sbuf(P2 + "_sstg", (128, 512), BF16)
    eps5 = k.sbuf(P2 + "_eps5", (1, 1), F32)
    xt2 = k.sbuf(P2 + "_xt2", (128, D), F32)
    zsh2 = k.sbuf(P2 + "_zsh2", (128, 512), BF16)
    pA2 = k.psum(P2 + "_pA", (128, 512), F32)
    pB2 = k.psum(P2 + "_pB", (128, 512), F32)
    pC2 = k.psum(P2 + "_pC", (128, 512), F32)
    k.dma("sp", b1c[:, :], io[f"b1_cols{l}"].ap(), P2 + "c", writes=[(P2, "b1c")])
    k.dma("sp", lgc[:, 0, :], io[f"lg_cols{l}"].ap(), P2 + "c", writes=[(P2, "lgc")])
    k.dma("sp", lgc[:, 1, :], io[f"lb_cols{l}"].ap(), P2 + "c", writes=[(P2, "lgc")])
    k.dma("pool", b2b[:, :], io[f"b2_row{l}"].ap().partition_broadcast(128), P2 + "c2",
          writes=[(P2, "b2b")])
    k.memset("dve", eps5[:, :], 1e-5, writes=[(P2, "eps5")])
    for hc in range(22):
        rows = 128 if hc < 21 else 42
        k.dma("pool", w2[0:rows, hc, :], io[f"w2_{l}"].ap()[128 * hc:128 * hc + rows, :],
              P2 + "w2", writes=[(P2, "w2", hc)])
    x2_fm = k.sbuf(P2 + "_x2fm", (128, NCH, S), BF16)
    for c in range(4):
        k.dma("sp", x2_fm[:, c, :], io["xi_bf16"].ap()[:, 128 * c:128 * (c + 1)], P2 + "x",
              writes=[(P2 + "_x2fm", c)], transpose=True)
    for c in range(4, NCH):
        k.dma("sp", x2_fm[:, c, :], io["xi_sh"].ap()[:, 128 * (c - 4):128 * (c - 3)], P2 + "x",
              writes=[(P2 + "_x2fm", c)], transpose=True)
    pMa = k.psum(P2 + "_pMa", (1, 512), F32)
    pMb = k.psum(P2 + "_pMb", (1, 512), F32)
    for sp_ in range(8):
        ssl = slice(512 * sp_, 512 * (sp_ + 1))
        for c in range(NCH):
            k.mm(pMa[0:1, :], OC[:, :], x2_fm[:, c, ssl], c == 0, c == NCH - 1,
                 reads=[(P2 + "_x2fm", c), ("ones_col_b",)], writes=[(P2, "pMa")])
            k.act(sstg2[:, :], x2_fm[:, c, ssl], AF.Square, reads=[(P2 + "_x2fm", c)],
                  writes=[(P2, "sstg")])
            k.mm(pMb[0:1, :], OC[:, :], sstg2[:, :], c == 0, c == NCH - 1,
                 reads=[(P2, "sstg"), ("ones_col_b",)], writes=[(P2, "pMb")])
        k.ts("dve", mrow[0:1, :], pMa[0:1, :], 1.0 / D, None, ALU.mult,
             reads=[(P2, "pMa")], writes=[(P2, "mrow")])
        k.ts("dve", rrow[0:1, :], pMb[0:1, :], 1.0 / D, None, ALU.mult,
             reads=[(P2, "pMb")], writes=[(P2, "rrow")])
        k.tt("dve", astg[0:1, 0:512], mrow[0:1, :], mrow[0:1, :], ALU.mult,
             reads=[(P2, "mrow")], writes=[(P2, "astg")])
        k.tt("dve", rrow[0:1, :], rrow[0:1, :], astg[0:1, 0:512], ALU.subtract,
             reads=[(P2, "rrow"), (P2, "astg")], writes=[(P2, "rrow")])
        k.act(rrow[0:1, :], rrow[0:1, :], AF.Ln, bias=eps5[0:1, 0:1],
              reads=[(P2, "rrow"), (P2, "eps5")], writes=[(P2, "rrow")])
        k.act(rrow[0:1, :], rrow[0:1, :], AF.Exp, scale=-0.5, reads=[(P2, "rrow")], writes=[(P2, "rrow")])
        k.mm(pA2[:, :], OR[:, :], mrow[0:1, :], True, True,
             reads=[(P2, "mrow"), ("ones_row_f",)], writes=[(P2, "pA")])
        k.mm(pB2[:, :], OR[:, :], rrow[0:1, :], True, True,
             reads=[(P2, "rrow"), ("ones_row_f",)], writes=[(P2, "pB")])
        for c in range(NCH):
            k.tt("dve", astg[:, :], x2_fm[:, c, ssl], pA2[:, :], ALU.subtract,
                 reads=[(P2 + "_x2fm", c), (P2, "pA")], writes=[(P2, "astg")])
            k.stt("dve", astg[:, :], astg[:, :], lgc[:, 0, c:c + 1], pB2[:, :],
                  ALU.mult, ALU.mult, reads=[(P2, "astg"), (P2, "lgc"), (P2, "pB")],
                  writes=[(P2, "astg")])
            k.ts("dve", h2n[:, c, ssl], astg[:, :], lgc[:, 1, c:c + 1], None, ALU.add,
                 reads=[(P2, "astg"), (P2, "lgc")], writes=[(P2, "h2n", c)])
    k.free(P2 + "_x2fm")
    import os as _o2
    if _o2.environ.get("L0F_DUMPLN") and l == 0:
        dbgq = k.dram("dbgq", (128, NCH, S), BF16, kind="ExternalOutput")
        k.barrier()
        k.dma("sp", dbgq.ap(), h2n[:, :, :], P2 + "dln")

    h4 = k.sbuf(P2 + "_h4", (128, 22, 1024), BF16)
    w1s = [k.sbuf(P2 + f"_w1s{i}", (128, NCH, 128), BF16) for i in range(2)]
    OCROWS = [128] * 42 + [42, 42]
    OCOFF = [0]
    for r_ in OCROWS:
        OCOFF.append(OCOFF[-1] + r_)
    for big in range(4):
        bsl0 = 1024 * big
        for pr in range(22):
            rows = OCROWS[2 * pr]
            for i01 in range(2):
                oc = 2 * pr + i01
                k.dma("pool", w1s[i01][:, :, 0:rows],
                      io[f"w1_{l}"].ap()[:, OCOFF[oc]:OCOFF[oc] + rows].rearrange("(c p) d -> p c d", p=128),
                      P2 + "w1l" + str(i01), writes=[(P2, "w1s", i01)])
            for sub in range(2):
                ssl = slice(bsl0 + 512 * sub, bsl0 + 512 * (sub + 1))
                for c in range(NCH):
                    k.mm(pA2[0:rows, :], w1s[0][:, c, 0:rows], h2n[:, c, ssl],
                         c == 0, c == NCH - 1,
                         reads=[(P2, "w1s", 0), (P2, "h2n", c)], writes=[(P2, "pA")])
                k.ts("dve", astg[0:rows, :], pA2[0:rows, :], b1c[0:rows, 2 * pr:2 * pr + 1], None,
                     ALU.add, reads=[(P2, "pA"), (P2, "b1c")], writes=[(P2, "astg")])
                for c in range(NCH):
                    k.mm(pB2[0:rows, :], w1s[1][:, c, 0:rows], h2n[:, c, ssl],
                         c == 0, c == NCH - 1,
                         reads=[(P2, "w1s", 1), (P2, "h2n", c)], writes=[(P2, "pB")])
                k.act(sstg2[0:rows, :], pB2[0:rows, :], AF.Gelu, bias=b1c[0:rows, 2 * pr + 1:2 * pr + 2],
                      reads=[(P2, "pB"), (P2, "b1c")], writes=[(P2, "sstg")])
                k.tt("dve", h4[0:rows, pr, 512 * sub:512 * (sub + 1)], astg[0:rows, :],
                     sstg2[0:rows, :], ALU.mult,
                     reads=[(P2, "astg"), (P2, "sstg")], writes=[(P2, "h4", pr)])
        import os as _o3
        if _o3.environ.get("L0F_DUMPH4") and l == 0 and big == 0:
            dbgq = k.dram("dbgq", (128, 22, 1024), BF16, kind="ExternalOutput")
            k.barrier()
            k.dma("sp", dbgq.ap()[:, 0:21, :], h4[:, 0:21, :], P2 + "dh4")
            k.dma("sp", dbgq.ap()[0:42, 21:22, :], h4[0:42, 21:22, :], P2 + "dh4")
        for tsub in range(8):
            trow = bsl0 + 128 * tsub
            k.dma("sp", xt2[:, :], io["xi_f32"].ap()[trow:trow + 128, :], P2 + "xt", writes=[(P2, "xt2")])
            for half in range(2):
                hsl = slice(512 * half, 512 * (half + 1))
                for pr in range(22):
                    rows = OCROWS[2 * pr]
                    k.mm(pC2[:, :], h4[0:rows, pr, 128 * (tsub % 8):128 * (tsub % 8 + 1)], w2[0:rows, pr, hsl],
                         pr == 0, pr == 21, reads=[(P2, "h4", pr), (P2, "w2", pr)], writes=[(P2, "pC")])
                k.tt("dve", xt2[:, hsl], xt2[:, hsl], pC2[:, :], ALU.add,
                     reads=[(P2, "xt2"), (P2, "pC")], writes=[(P2, "xt2")])
                k.tt("dve", xt2[:, hsl], xt2[:, hsl], b2b[:, hsl], ALU.add,
                     reads=[(P2, "xt2"), (P2, "b2b")], writes=[(P2, "xt2")])
            k.dma("sp", io["xi_f32"].ap()[trow:trow + 128, :], xt2[:, :], P2 + "xo", reads=[(P2, "xt2")])
            if l == 0:
                k.dma("pool", io["xi_bf16"].ap()[trow:trow + 128, :], xt2[:, :], P2 + "xob",
                      reads=[(P2, "xt2")])
                k.copy("dve", zsh2[:, :], xt2[:, 512:1024], reads=[(P2, "xt2")], writes=[(P2, "zsh2")])
                lo = trow + 1
                hi = min(trow + 129, S)
                k.dma("pool", io["xi_sh"].ap()[lo:hi, :], zsh2[0:hi - lo, :], P2 + "xos",
                      reads=[(P2, "zsh2")])
    k.free(P2 + "_h2n", P2 + "_w2", P2 + "_h4", P2 + "_astg", P2 + "_w1s0", P2 + "_w1s1",
           P2 + "_b1c", P2 + "_lgc", P2 + "_b2b", P2 + "_mrow", P2 + "_rrow", P2 + "_sstg",
           P2 + "_eps5", P2 + "_xt2", P2 + "_zsh2",
           P2 + "_pA", P2 + "_pB", P2 + "_pC", P2 + "_pMa", P2 + "_pMb")


def emit_l0_ffn(k, io, upto=None):
    P = "l0f"
    xi_f32 = k.dram(P + "_xi_f32", (S, D), F32)
    xi_bf16 = k.dram(P + "_xi_bf16", (S, D), BF16)
    xi_sh = k.dram(P + "_xi_sh", (S, 512), BF16)
    k.dma("sp", xi_f32.ap(), io["x_f32"].ap(), P + "init", writes=[("l0f_xi",)])
    k.dma("sp", xi_bf16.ap(), io["x_bf16"].ap(), P + "init", writes=[("l0f_xib",)])
    k.dma("sp", xi_sh.ap()[1:S, :], io["x_bf16"].ap()[0:S - 1, 512:1024], P + "init",
          writes=[("l0f_xis",)])
    k.dma("sp", xi_sh.ap()[0:1, :], k._sbufs["zrow_b"][0:1, :], P + "init",
          reads=[("zrow_b",)], writes=[("l0f_xis",)])
    k.barrier()
    io2 = dict(io)
    io2["xi_f32"], io2["xi_bf16"], io2["xi_sh"] = xi_f32, xi_bf16, xi_sh

    def _dump():
        dbg = k.dram("dbg", (S, D), F32, kind="ExternalOutput")
        k.barrier()
        k.dma("sp", dbg.ap(), xi_f32.ap(), "l0fdbg")
    emit_l0_ffn_layer(k, io2, 0, part1_only=(upto == "p1l0"))
    if upto in ("p1l0", "p2l0"):
        _dump(); return
    k.barrier()
    emit_l0_ffn_layer(k, io2, 1, part1_only=(upto == "p1l1"))
    if upto in ("p1l1", "p2l1"):
        _dump(); return
    k.barrier()
    zt = k.sbuf(P + "_zt", (128, D), F32)
    z2 = k.sbuf(P + "_z2", (128, D), F32)
    yt = k.sbuf(P + "_yt", (128, D), F32)
    ln = LNKit(k, P + "_ln")
    for tt_ in range(32):
        tsl = slice(128 * tt_, 128 * (tt_ + 1))
        k.dma("sp", zt[:, :], io["x_f32"].ap()[tsl, :], P + "ye1", writes=[(P, "zt")])
        k.dma("sp", z2[:, :], xi_f32.ap()[tsl, :], P + "ye2", writes=[(P, "z2")])
        k.tt("dve", zt[:, :], zt[:, :], z2[:, :], ALU.add,
             reads=[(P, "zt"), (P, "z2")], writes=[(P, "zt")])
        ln.apply(zt[:, :], yt[:, :], (P, "zt"), (P, "yt"))
        k.dma("sp", io["y_f32"].ap()[tsl, :], yt[:, :], P + "y", reads=[(P, "yt")])
        k.dma("pool", io["y_bf16"].ap()[tsl, :], yt[:, :], P + "yb", reads=[(P, "yt")])
    ln.free()
    k.free(P + "_zt", P + "_z2", P + "_yt")



def host_consts_l1_ffn(k):
    s_ = np.arange(128)[:, None]
    t_ = np.arange(128)[None, :]
    k.const("c_l1f_ltriS", _bf16((s_ < t_).astype(np.float64)))


def emit_l1_ffn(k, io, upto=None):
    P = "l1f"
    ORF = k._sbufs["ones_row_f"]
    OCB = k._sbufs["ones_col_b"]
    OCF = k._sbufs["ones_col_f"]
    IDF = k._sbufs["ident_f"]

    kv_my = k.dram(P + "_kv_my", (S, 2048), BF16)
    kv_all = k.dram(P + "_kv_all", (8 * S, 2048), BF16)
    q_dram = k.dram(P + "_q_dram", (S, D), BF16)
    o_dram = k.dram(P + "_o_dram", (S, D), BF16)
    y1_f32 = k.dram(P + "_y1f", (S, D), F32)
    y1_bf16 = k.dram(P + "_y1b", (S, D), BF16)
    cnt_my = k.dram(P + "_cnt_my", (1, 8), F32)
    cnt_all = k.dram(P + "_cnt_all", (8, 8), F32)
    offd = k.dram(P + "_offd", (1, 8), F32)
    w1bf = k.dram(P + "_w1bf", (D, 4096), BF16)
    w2bf = k.dram(P + "_w2bf", (4096, D), BF16)
    cmy = k.dram(P + "_cmy", (1025, D), BF16)
    call = k.dram(P + "_call", (8 * 1025, D), BF16)
    w8d = k.dram(P + "_w8d", (8, S), BF16)
    y2hi = k.dram(P + "_y2hi", (D, S), BF16)
    y2lo = k.dram(P + "_y2lo", (D, S), BF16)

    pA = k.psum(P + "_pA", (128, 512), F32)
    pB = k.psum(P + "_pB", (128, 512), F32)

    x_fm = load_fm(k, P + "_xfm", io["x_bf16"], P + "x")
    wt = load_wT(k, P + "_wti", io["in_proj_w"], P + "wi", dout=3072, din=D)
    brow = k.sbuf(P + "_brow", (1, 3072), F32)
    bb = k.sbuf(P + "_bb", (128, 6, 512), F32)
    qkvs = k.sbuf(P + "_qkvs", (128, 6, 512), BF16)
    k.dma("sp", brow[0:1, :], io["in_proj_b"].ap(), P + "c", writes=[(P, "brow")])
    for j in range(6):
        ps, pk_ = (pA, (P, "pA")) if j % 2 == 0 else (pB, (P, "pB"))
        k.mm(ps[:, :], ORF[:, :], brow[0:1, 512 * j:512 * (j + 1)], True, True,
             reads=[("ones_row_f",), (P, "brow")], writes=[pk_])
        k.copy("dve", bb[:, j, :], ps[:, :], reads=[pk_], writes=[(P, "bb", j)])
    for tt in range(32):
        tsl = slice(128 * tt, 128 * (tt + 1))
        for j in range(6):
            ps, pk_ = (pA, (P, "pA")) if j % 2 == 0 else (pB, (P, "pB"))
            for c in range(NCH):
                k.mm(ps[:, :], x_fm[:, c, tsl], wt[:, c, 512 * j:512 * (j + 1)],
                     c == 0, c == NCH - 1, reads=[(P + "_xfm", c), (P + "_wti", c)], writes=[pk_])
            k.tt("dve", qkvs[:, j, :], ps[:, :], bb[:, j, :], ALU.add,
                 reads=[pk_, (P, "bb", j)], writes=[(P, "qkvs", j)])
        k.dma("sp", q_dram.ap()[tsl, :], qkvs[:, 0:2, :].rearrange("p a b -> p (a b)"),
              P + "qo", reads=[(P, "qkvs", 0), (P, "qkvs", 1)], writes=[(P, "qd")])
        k.dma("sp", kv_my.ap()[tsl, :], qkvs[:, 2:6, :].rearrange("p a b -> p (a b)"),
              P + "kvo", reads=[(P, "qkvs", 2), (P, "qkvs", 3), (P, "qkvs", 4), (P, "qkvs", 5)],
              writes=[(P, "kvd")])
    k.free(P + "_qkvs", P + "_bb", P + "_brow", P + "_wti", P + "_xfm")
    k.collective("AllGather", ALU.bypass, [kv_my.ap()], [kv_all.ap()], P + "kv",
                 reads=[(P, "kvd")], writes=[(P, "kvall")])

    k.dma("pool", w1bf.ap(), io["my_w1"].ap(), P + "wc", writes=[(P, "w1bf")])
    k.dma("pool", w2bf.ap(), io["my_w2"].ap(), P + "wc", writes=[(P, "w2bf")])
    w1t = k.sbuf(P + "_w1t", (128, 32, D), BF16)
    for f in range(32):
        k.dma("sp", w1t[:, f, :], w1bf.ap()[:, 128 * f:128 * (f + 1)], P + "w1t",
              reads=[(P, "w1bf")], writes=[(P, "w1t", f)], transpose=True)
    w2res = k.sbuf(P + "_w2res", (128, 32, 512), BF16)
    b1c = k.sbuf(P + "_b1c", (128, 32), F32)
    b1cb = k.sbuf(P + "_b1cb", (128, 32), BF16)
    b2row = k.sbuf(P + "_b2row", (1, D), F32)
    cstg = [k.sbuf(P + f"_cstg{i}", (128, 512), BF16) for i in range(2)]
    k.dma("sp", b1c[:, :], io["my_b1_cols"].ap(), P + "c", writes=[(P, "b1c")])
    k.dma("sp", b2row[0:1, :], io["my_b2_row"].ap(), P + "c", writes=[(P, "b2row")])
    k.copy("dve", b1cb[:, :], b1c[:, :], reads=[(P, "b1c")], writes=[(P, "b1cb")])
    for half in range(2):
        hsl = slice(512 * half, 512 * (half + 1))
        for f in range(32):
            k.dma("sp", w2res[:, f, :], w2bf.ap()[128 * f:128 * (f + 1), hsl], P + "w2l",
                  reads=[(P, "w2bf")], writes=[(P, "w2res", f)])
        for dpc in range(8):
            ps, pk_ = (pA, (P, "pA")) if dpc % 2 == 0 else (pB, (P, "pB"))
            for f in range(32):
                k.mm(ps[:, :], w1t[:, f, 128 * dpc:128 * (dpc + 1)], w2res[:, f, :],
                     f == 0, f == 31, reads=[(P, "w1t", f), (P, "w2res", f)], writes=[pk_])
            jc = dpc % 2
            k.copy("dve", cstg[jc][:, :], ps[:, :], reads=[pk_], writes=[(P, "cstg", jc)])
            k.dma("sp", cmy.ap()[128 * dpc:128 * (dpc + 1), hsl], cstg[jc][:, :], P + "co",
                  reads=[(P, "cstg", jc)], writes=[(P, "cmyd")])
        for f in range(32):
            k.mm(pA[0:1, :], b1cb[:, f:f + 1], w2res[:, f, :], f == 0, f == 31,
                 reads=[(P, "b1cb"), (P, "w2res", f)], writes=[(P, "pA")])
        k.tt("dve", cstg[0][0:1, :], pA[0:1, :], b2row[0:1, hsl], ALU.add,
             reads=[(P, "pA"), (P, "b2row")], writes=[(P, "cstg", 0)])
        k.dma("sp", cmy.ap()[1024:1025, hsl], cstg[0][0:1, :], P + "co",
              reads=[(P, "cstg", 0)], writes=[(P, "cmyd")])
    k.collective("AllGather", ALU.bypass, [cmy.ap()], [call.ap()], P + "cc",
                 reads=[(P, "cmyd")], writes=[(P, "call")])
    if upto == "cmy":
        k.dma("sp", io["dbg"].ap(), cmy.ap(), P + "dbg", reads=[(P, "cmyd")], writes=[(P, "dbg")])
        return

    k8 = [k.sbuf(P + "_k80", (128, 8, D), BF16)] * 2
    v8 = [k.sbuf(P + "_v80", (128, 8, D), BF16)] * 2
    qtb = [k.sbuf(P + f"_qt{i}", (128, D), BF16) for i in range(2)]
    pall = k.sbuf(P + "_pall", (128, 8, D), F32)
    scm = k.sbuf(P + "_scm", (128, 8, 4), F32)
    mx = k.sbuf(P + "_mx", (128, 4, 1), F32)
    at8 = k.sbuf(P + "_at8", (128, 4, 8), F32)
    ost = k.sbuf(P + "_ost", (128, D), BF16)
    kv_v = kv_all.ap().rearrange("(m s) d -> s m d", m=8)
    for tt in range(32):
        tsl = slice(128 * tt, 128 * (tt + 1))
        jq = tt % 2
        k.dma("sp", qtb[jq][:, :], q_dram.ap()[tsl, :], P + "ql",
              reads=[(P, "qd")], writes=[(P, "qt", jq)])
        k.dma("sp", k8[jq][:, :, :], kv_v[tsl, :, 0:D], P + "kl",
              reads=[(P, "kvall")], writes=[(P, "km")])
        k.dma("pool", v8[jq][:, :, :], kv_v[tsl, :, D:2 * D], P + "vl",
              reads=[(P, "kvall")], writes=[(P, "vm")])
        qb = bcast_at(qtb[jq][:, :], 8, 1)
        k.tt("dve", pall[:, 0:5, :], k8[jq][:, 0:5, :], bcast_at(qtb[jq][:, :], 5, 1),
             ALU.mult, reads=[(P, "km"), (P, "qt", jq)], writes=[(P, "pall", 0)])
        k.tt("pool", pall[:, 5:8, :], k8[jq][:, 5:8, :], bcast_at(qtb[jq][:, :], 3, 1),
             ALU.mult, reads=[(P, "km"), (P, "qt", jq)], writes=[(P, "pall", 1)])
        k.red(scm[:, 0:5, :], pall[:, 0:5, :].rearrange("p m (h d) -> p m h d", h=4),
              AX.X, ALU.add, reads=[(P, "pall", 0)], writes=[(P, "scm")])
        k.red(scm[:, 5:8, :], pall[:, 5:8, :].rearrange("p m (h d) -> p m h d", h=4),
              AX.X, ALU.add, reads=[(P, "pall", 1)], writes=[(P, "scm")])
        scr = scm[:, :, :].rearrange("p m h -> p h m")
        k.red(mx[:, :, 0:1], scr, AX.X, ALU.max, reads=[(P, "scm")], writes=[(P, "mx")])
        k.tt("dve", at8[:, :, :], scr, bcast_free(mx[:, :, 0], 8), ALU.subtract,
             reads=[(P, "scm"), (P, "mx")], writes=[(P, "at8")])
        k.act(at8[:, :, :], at8[:, :, :], AF.Exp, scale=0.0625,
              reads=[(P, "at8")], writes=[(P, "at8")])
        k.red(mx[:, :, 0:1], at8[:, :, :], AX.X, ALU.add, reads=[(P, "at8")], writes=[(P, "mx")])
        k.recip(mx[:, :, 0:1], mx[:, :, 0:1], reads=[(P, "mx")], writes=[(P, "mx")])
        k.ts("dve", at8[:, :, :], at8[:, :, :], mx[:, 0, 0:1], None, ALU.mult,
             reads=[(P, "at8"), (P, "mx")], writes=[(P, "at8")]) if False else \
        k.tt("dve", at8[:, :, :], at8[:, :, :], bcast_free(mx[:, :, 0], 8), ALU.mult,
             reads=[(P, "at8"), (P, "mx")], writes=[(P, "at8")])
        atb = at8[:, :, :].rearrange("p h m -> p m h")
        k.tt("dve", pall[:, 0:4, :].rearrange("p m (h d) -> p m h d", h=4),
             v8[jq][:, 0:4, :].rearrange("p m (h d) -> p m h d", h=4),
             bcast_free(atb[:, 0:4, :], 256), ALU.mult,
             reads=[(P, "vm"), (P, "at8")], writes=[(P, "pall", 0)])
        k.tt("pool", pall[:, 4:8, :].rearrange("p m (h d) -> p m h d", h=4),
             v8[jq][:, 4:8, :].rearrange("p m (h d) -> p m h d", h=4),
             bcast_free(atb[:, 4:8, :], 256), ALU.mult,
             reads=[(P, "vm"), (P, "at8")], writes=[(P, "pall", 1)])
        k.tt("dve", pall[:, 0:4, :], pall[:, 0:4, :], pall[:, 4:8, :], ALU.add,
             reads=[(P, "pall", 0), (P, "pall", 1)], writes=[(P, "pall", 0)])
        k.tt("pool", pall[:, 0:2, :], pall[:, 0:2, :], pall[:, 2:4, :], ALU.add,
             reads=[(P, "pall", 0)], writes=[(P, "pall", 0)])
        k.tt("dve", ost[:, :], pall[:, 0, :], pall[:, 1, :], ALU.add,
             reads=[(P, "pall", 0)], writes=[(P, "ost")])
        k.dma("sp", o_dram.ap()[tsl, :], ost[:, :], P + "oo", reads=[(P, "ost")], writes=[(P, "od")])
    k.free(P + "_ost", P + "_at8", P + "_mx", P + "_scm", P + "_pall",
           P + "_qt0", P + "_qt1", P + "_k80", P + "_v80",
           P + "_cstg0", P + "_cstg1", P + "_b2row", P + "_b1cb", P + "_b1c",
           P + "_w2res", P + "_w1t")
    if upto == "o":
        k.dma("sp", io["dbg"].ap(), o_dram.ap(), P + "dbg", reads=[(P, "od")], writes=[(P, "dbg")])
        return

    lnk = LNKit(k, P + "_lnk")
    owT = load_wT(k, P + "_owt", io["out_proj_w"], P + "wo", dout=D, din=D)
    o_fm = load_fm(k, P + "_ofm", o_dram, P + "ol", reads=[(P, "od")])
    rows = k.sbuf(P + "_rows", (1, 3, D), F32)
    obc = k.sbuf(P + "_obc", (128, D), F32)
    gbc = k.sbuf(P + "_gbc", (128, D), F32)
    bbc = k.sbuf(P + "_bbc", (128, D), F32)
    xres = [k.sbuf(P + f"_xres{i}", (128, D), F32) for i in range(2)]
    z = k.sbuf(P + "_z", (128, D), F32)
    y1t = k.sbuf(P + "_y1t", (128, D), F32)
    y1bt = k.sbuf(P + "_y1bt", (128, D), BF16)
    for i, nm in enumerate(("out_proj_b", "ln1_g", "ln1_b")):
        k.dma("sp", rows[0:1, i, :], io[nm].ap(), P + "c", writes=[(P, "rows", i)])
    for i, tgt in enumerate((obc, gbc, bbc)):
        for half in range(2):
            ps, pk_ = (pA, (P, "pA")) if half == 0 else (pB, (P, "pB"))
            hsl = slice(512 * half, 512 * (half + 1))
            k.mm(ps[:, :], ORF[:, :], rows[0:1, i, hsl], True, True,
                 reads=[("ones_row_f",), (P, "rows", i)], writes=[pk_])
            k.copy("dve", tgt[:, hsl], ps[:, :], reads=[pk_], writes=[(P, "bct", i)])
    for tt in range(32):
        tsl = slice(128 * tt, 128 * (tt + 1))
        jx = tt % 2
        k.dma("sp", xres[jx][:, :], io["x_f32"].ap()[tsl, :], P + "xr", writes=[(P, "xr", jx)])
        for half in range(2):
            ps, pk_ = (pA, (P, "pA")) if half == 0 else (pB, (P, "pB"))
            hsl = slice(512 * half, 512 * (half + 1))
            for c in range(NCH):
                k.mm(ps[:, :], o_fm[:, c, tsl], owT[:, c, hsl], c == 0, c == NCH - 1,
                     reads=[(P + "_ofm", c), (P + "_owt", c)], writes=[pk_])
            k.tt("dve", z[:, hsl], ps[:, :], obc[:, hsl], ALU.add,
                 reads=[pk_, (P, "bct", 0)], writes=[(P, "z")])
        k.tt("dve", z[:, :], z[:, :], xres[jx][:, :], ALU.add,
             reads=[(P, "z"), (P, "xr", jx)], writes=[(P, "z")])
        lnk.apply(z[:, :], z[:, :], (P, "z"), (P, "z"))
        k.tt("dve", z[:, :], z[:, :], gbc[:, :], ALU.mult,
             reads=[(P, "z"), (P, "bct", 1)], writes=[(P, "z")])
        k.tt("dve", y1t[:, :], z[:, :], bbc[:, :], ALU.add,
             reads=[(P, "z"), (P, "bct", 2)], writes=[(P, "y1t")])
        k.dma("sp", y1_f32.ap()[tsl, :], y1t[:, :], P + "y1o",
              reads=[(P, "y1t")], writes=[(P, "y1f")])
        k.copy("dve", y1bt[:, :], y1t[:, :], reads=[(P, "y1t")], writes=[(P, "y1bt")])
        k.dma("sp", y1_bf16.ap()[tsl, :], y1bt[:, :], P + "y1o",
              reads=[(P, "y1bt")], writes=[(P, "y1b")])
    k.free(P + "_y1bt", P + "_y1t", P + "_z", P + "_xres0", P + "_xres1",
           P + "_bbc", P + "_gbc", P + "_obc", P + "_rows", P + "_ofm", P + "_owt")
    if upto == "y1":
        k.dma("sp", io["dbg"].ap(), y1_f32.ap(), P + "dbg", reads=[(P, "y1f")], writes=[(P, "dbg")])
        return
    k.free(P + "_pB", P + "_pA")

    y1_fm = load_fm(k, P + "_y1fm", y1_bf16, P + "y1l", reads=[(P, "y1b")])
    pF = k.psum(P + "_pF", (128, 8, 512), F32)
    wgs = k.sbuf(P + "_wgs", (128, 8, 8), F32)
    wgb = k.sbuf(P + "_wgb", (128, 8, 8), BF16)
    lts = k.sbuf(P + "_lts", (128, 128), BF16)
    ohsf = k.sbuf(P + "_ohsf", (128, 32, 8), F32)
    gvs = k.sbuf(P + "_gvs", (128, 32), F32)
    locs = k.sbuf(P + "_locs", (128, 32), F32)
    runrow = k.sbuf(P + "_runrow", (1, 8), F32)
    runbcs = k.sbuf(P + "_runbcs", (128, 8), F32)
    g8 = k.sbuf(P + "_g8", (128, 8), F32)
    gcol = k.sbuf(P + "_gcol", (128, 1), F32)
    es8 = k.sbuf(P + "_es8", (128, 8), F32)
    cum = k.sbuf(P + "_cum", (128, 3, 8), F32)
    ohb = k.sbuf(P + "_ohb", (128, 8), BF16)
    la8 = k.sbuf(P + "_la8", (128, 8), F32)
    ls8 = k.sbuf(P + "_ls8", (128, 8), F32)
    for c in range(NCH):
        k.dma("sp", wgs[:, c, :], io["wg"].ap()[128 * c:128 * (c + 1), :], P + "c2",
              writes=[(P, "wgs")])
    k.copy("dve", wgb[:, :, :], wgs[:, :, :], reads=[(P, "wgs")], writes=[(P, "wgb")])
    k.dma("sp", lts[:, :], k._drams["c_l1f_ltriS"].ap(), P + "c2", writes=[(P, "lts")])
    k.memset("dve", runrow[0:1, :], 0.0, writes=[(P, "runrow")])
    for tt in range(32):
        tsl = slice(128 * tt, 128 * (tt + 1))
        for c in range(NCH):
            k.mm(pF[:, 0, 0:8], y1_fm[:, c, tsl], wgb[:, c, :], c == 0, c == NCH - 1,
                 reads=[(P + "_y1fm", c), (P, "wgb")], writes=[(P, "pF", 0)])
        k.copy("dve", g8[:, :], pF[:, 0, 0:8], reads=[(P, "pF", 0)], writes=[(P, "g8")])
        k.red(gcol[:, 0:1], g8[:, :], AX.X, ALU.max, reads=[(P, "g8")], writes=[(P, "gcol")])
        k.ts("dve", es8[:, :], g8[:, :], gcol[:, 0:1], None, ALU.subtract,
             reads=[(P, "g8"), (P, "gcol")], writes=[(P, "es8")])
        k.act(es8[:, :], es8[:, :], AF.Exp, reads=[(P, "es8")], writes=[(P, "es8")])
        k.red(gcol[:, 0:1], es8[:, :], AX.X, ALU.add, reads=[(P, "es8")], writes=[(P, "gcol")])
        k.recip(gcol[:, 0:1], gcol[:, 0:1], reads=[(P, "gcol")], writes=[(P, "gcol")])
        k.ts("dve", g8[:, :], es8[:, :], gcol[:, 0:1], None, ALU.mult,
             reads=[(P, "es8"), (P, "gcol")], writes=[(P, "g8")])
        k.red(gvs[:, tt:tt + 1], g8[:, :], AX.X, ALU.max, reads=[(P, "g8")], writes=[(P, "gvs")])
        k.ts("dve", es8[:, :], g8[:, :], gvs[:, tt:tt + 1], None, ALU.is_equal,
             reads=[(P, "g8"), (P, "gvs")], writes=[(P, "es8")])
        k.copy("dve", cum[:, 0, 0:1], es8[:, 0:1], reads=[(P, "es8")], writes=[(P, "cum")])
        k.tt("dve", cum[:, 0, 1:8], es8[:, 1:8], es8[:, 0:7], ALU.add,
             reads=[(P, "es8")], writes=[(P, "cum")])
        k.copy("dve", cum[:, 1, 0:2], cum[:, 0, 0:2], reads=[(P, "cum")], writes=[(P, "cum")])
        k.tt("dve", cum[:, 1, 2:8], cum[:, 0, 2:8], cum[:, 0, 0:6], ALU.add,
             reads=[(P, "cum")], writes=[(P, "cum")])
        k.copy("dve", cum[:, 2, 0:4], cum[:, 1, 0:4], reads=[(P, "cum")], writes=[(P, "cum")])
        k.tt("dve", cum[:, 2, 4:8], cum[:, 1, 4:8], cum[:, 1, 0:4], ALU.add,
             reads=[(P, "cum")], writes=[(P, "cum")])
        k.tt("dve", cum[:, 0, :], cum[:, 2, :], es8[:, :], ALU.subtract,
             reads=[(P, "cum"), (P, "es8")], writes=[(P, "cum")])
        k.ts("dve", cum[:, 1, :], cum[:, 0, :], 0.0, None, ALU.is_equal,
             reads=[(P, "cum")], writes=[(P, "cum")])
        k.tt("dve", ohsf[:, tt, :], es8[:, :], cum[:, 1, :], ALU.mult,
             reads=[(P, "es8"), (P, "cum")], writes=[(P, "ohsf", tt)])
        k.copy("dve", ohb[:, :], ohsf[:, tt, :], reads=[(P, "ohsf", tt)], writes=[(P, "ohb")])
        k.mm(pF[:, 1, 0:8], lts[:, :], ohb[:, :], True, True,
             reads=[(P, "lts"), (P, "ohb")], writes=[(P, "pF", 1)])
        k.mm(pF[:, 2, 0:8], ORF[:, :], runrow[0:1, :], True, True,
             reads=[("ones_row_f",), (P, "runrow")], writes=[(P, "pF", 2)])
        k.copy("dve", runbcs[:, :], pF[:, 2, 0:8], reads=[(P, "pF", 2)], writes=[(P, "runbcs")])
        k.tt("dve", la8[:, :], pF[:, 1, 0:8], runbcs[:, :], ALU.add,
             reads=[(P, "pF", 1), (P, "runbcs")], writes=[(P, "la8")])
        k.tt("dve", ls8[:, :], la8[:, :], ohsf[:, tt, :], ALU.mult,
             reads=[(P, "la8"), (P, "ohsf", tt)], writes=[(P, "ls8")])
        k.red(locs[:, tt:tt + 1], ls8[:, :], AX.X, ALU.add, reads=[(P, "ls8")], writes=[(P, "locs")])
        k.mm(pF[0:1, 3, 0:8], OCB[:, 0:1], ohb[:, :], True, True,
             reads=[("ones_col_b",), (P, "ohb")], writes=[(P, "pF", 3)])
        k.tt("dve", runrow[0:1, :], runrow[0:1, :], pF[0:1, 3, 0:8], ALU.add,
             reads=[(P, "runrow"), (P, "pF", 3)], writes=[(P, "runrow")])
    k.dma("sp", cnt_my.ap(), runrow[0:1, :], P + "cnt", reads=[(P, "runrow")], writes=[(P, "cntd")])
    k.collective("AllGather", ALU.bypass, [cnt_my.ap()], [cnt_all.ap()], P + "cg",
                 reads=[(P, "cntd")], writes=[(P, "cnta")])
    ca = k.sbuf(P + "_ca", (8, 8), F32)
    pm = k.sbuf(P + "_pm", (8, 1), F32)
    oc8 = k.sbuf(P + "_oc8", (8, 1), F32)
    offb = k.sbuf(P + "_offb", (128, 8), F32)
    k.dma("sp", ca[:, :], cnt_all.ap(), P + "cnt", reads=[(P, "cnta")], writes=[(P, "ca")])
    k.dma("sp", pm[:, :], io["pmask"].ap(), P + "cnt", writes=[(P, "pm")])
    k.mm(pF[0:8, 6, 0:1], ca[:, :], pm[:, :], True, True,
         reads=[(P, "ca"), (P, "pm")], writes=[(P, "pF", 6)])
    k.copy("dve", oc8[:, 0:1], pF[0:8, 6, 0:1], reads=[(P, "pF", 6)], writes=[(P, "oc8")])
    k.dma("sp", offd.ap().rearrange("a b -> b a"), oc8[:, 0:1], P + "cnt",
          reads=[(P, "oc8")], writes=[(P, "offd")])
    k.dma("sp", offb[:, :], offd.ap().partition_broadcast(128), P + "cnt",
          reads=[(P, "offd")], writes=[(P, "offb")])
    off8 = k.sbuf(P + "_off8", (128, 8), F32)
    pos1 = k.sbuf(P + "_pos1", (128, 1), F32)
    wef = k.sbuf(P + "_wef", (128, 8), F32)
    w8stg = [k.sbuf(P + f"_w8stg{i}", (8, 128), BF16) for i in range(2)]
    for tt in range(32):
        tsl = slice(128 * tt, 128 * (tt + 1))
        k.tt("dve", off8[:, :], ohsf[:, tt, :], offb[:, :], ALU.mult,
             reads=[(P, "ohsf", tt), (P, "offb")], writes=[(P, "off8")])
        k.red(pos1[:, 0:1], off8[:, :], AX.X, ALU.add, reads=[(P, "off8")], writes=[(P, "pos1")])
        k.tt("dve", pos1[:, 0:1], pos1[:, 0:1], locs[:, tt:tt + 1], ALU.add,
             reads=[(P, "pos1"), (P, "locs")], writes=[(P, "pos1")])
        k.ts("dve", pos1[:, 0:1], pos1[:, 0:1], 4096.0, None, ALU.is_lt,
             reads=[(P, "pos1")], writes=[(P, "pos1")])
        k.tt("dve", pos1[:, 0:1], pos1[:, 0:1], gvs[:, tt:tt + 1], ALU.mult,
             reads=[(P, "pos1"), (P, "gvs")], writes=[(P, "pos1")])
        k.ts("dve", wef[:, :], ohsf[:, tt, :], pos1[:, 0:1], None, ALU.mult,
             reads=[(P, "ohsf", tt), (P, "pos1")], writes=[(P, "wef")])
        j = tt % 2
        k.transp(pF[0:8, 4 + j, 0:128], wef[:, :], IDF[:, :],
                 reads=[(P, "wef"), ("ident_f",)], writes=[(P, "pF", 4 + j)])
        k.copy("dve", w8stg[j][:, :], pF[0:8, 4 + j, 0:128],
               reads=[(P, "pF", 4 + j)], writes=[(P, "w8s", j)])
        k.dma("sp", w8d.ap()[:, tsl], w8stg[j][:, :], P + "w8o",
              reads=[(P, "w8s", j)], writes=[(P, "w8d")])
    if upto == "w8":
        k.dma("sp", io["dbg"].ap(), w8d.ap(), P + "dbg", reads=[(P, "w8d")], writes=[(P, "dbg")])
        return

    W8s = k.sbuf(P + "_W8s", (8, S), BF16)
    bias_sb = k.sbuf(P + "_biassb", (8, D), BF16)
    lgc = k.sbuf(P + "_lgc", (128, 8), F32)
    lbc = k.sbuf(P + "_lbc", (128, 8), F32)
    epsr = k.sbuf(P + "_epsr", (1, 1), F32)
    mrow = k.sbuf(P + "_mrow", (1, 512), F32)
    vrow = k.sbuf(P + "_vrow", (1, 512), F32)
    cbuf = [k.sbuf(P + f"_cbuf{i}", (128, 8, 1024), BF16) for i in range(3)]
    ystg = [k.sbuf(P + f"_ystg{i}", (128, 8, 512), BF16) for i in range(3)]
    wbc = [k.sbuf(P + f"_wbc{i}", (128, 512), BF16) for i in range(3)]
    zst = k.sbuf(P + "_zst", (128, 8, 512), F32)
    sqst = [k.sbuf(P + f"_sqst{i}", (128, 512), F32) for i in range(2)]
    ytmp = [k.sbuf(P + f"_ytmp{i}", (128, 512), F32) for i in range(2)]
    zapp = k.sbuf(P + "_zapp", (128, 512), F32)
    yhib = [k.sbuf(P + f"_yhib{i}", (128, 512), BF16) for i in range(2)]
    yhif = [k.sbuf(P + f"_yhif{i}", (128, 512), F32) for i in range(2)]
    ylob = [k.sbuf(P + f"_ylob{i}", (128, 512), BF16) for i in range(2)]
    k.dma("sp", W8s[:, :], w8d.ap(), P + "bl", reads=[(P, "w8d")], writes=[(P, "W8s")])
    for e in range(8):
        k.dma("sp", bias_sb[e:e + 1, :], call.ap()[1025 * e + 1024:1025 * e + 1025, :],
              P + "bl", reads=[(P, "call")], writes=[(P, "biassb")])
    k.dma("sp", lgc[:, :], io["ln2_g_col"].ap(), P + "bl", writes=[(P, "lgc")])
    k.dma("sp", lbc[:, :], io["ln2_b_col"].ap(), P + "bl", writes=[(P, "lbc")])
    k.memset("dve", epsr[0:1, 0:1], 1e-5, writes=[(P, "epsr")])
    for sp_ in range(8):
        ssl = slice(512 * sp_, 512 * (sp_ + 1))
        for e in range(8):
            j = e % 3
            k.dma("act" if e % 2 == 0 else "sp", cbuf[j][:, :, :],
                  call.ap()[1025 * e:1025 * e + 1024, :].rearrange("(dc p) d -> p dc d", p=128),
                  P + "cl" + str(j) + ("a" if e % 2 == 0 else "s"),
                  reads=[(P, "call")], writes=[(P, "cb", j)])
            k.dma("sp", wbc[j][:, :], w8d.ap()[e:e + 1, ssl].partition_broadcast(128),
                  P + "wb", reads=[(P, "w8d")], writes=[(P, "wbc", j)])
            for dc in range(8):
                k.tt("pool", ystg[j][:, dc, :], y1_fm[:, dc, ssl], wbc[j][:, :], ALU.mult,
                     reads=[(P + "_y1fm", dc), (P, "wbc", j)], writes=[(P, "ys", j, dc)])
            for dc in range(8):
                for oc in range(8):
                    k.mm(pF[:, oc, :], cbuf[j][:, dc, 128 * oc:128 * (oc + 1)],
                         ystg[j][:, dc, :], (e == 0 and dc == 0), False,
                         reads=[(P, "cb", j), (P, "ys", j, dc)], writes=[(P, "pF", oc)])
        for oc in range(8):
            k.mm(pF[:, oc, :], bias_sb[:, 128 * oc:128 * (oc + 1)], W8s[:, ssl],
                 False, True, reads=[(P, "biassb"), (P, "W8s")], writes=[(P, "pF", oc)])
        for oc in range(8):
            jt = oc % 2
            k.copy("pool", ytmp[jt][:, :], y1_fm[:, oc, ssl],
                   reads=[(P + "_y1fm", oc)], writes=[(P, "ytmp", jt)])
            k.tt("dve", zst[:, oc, :], pF[:, oc, :], ytmp[jt][:, :], ALU.add,
                 reads=[(P, "pF", oc), (P, "ytmp", jt)], writes=[(P, "zst", oc)])
        for oc in range(8):
            k.mm(pF[0:1, 0, :], OCF[:, 0:1], zst[:, oc, :], oc == 0, oc == 7,
                 reads=[("ones_col_f",), (P, "zst", oc)], writes=[(P, "pF", 0)])
        for oc in range(8):
            jt = oc % 2
            k.act(sqst[jt][:, :], zst[:, oc, :], AF.Square,
                  reads=[(P, "zst", oc)], writes=[(P, "sq", jt)])
            k.mm(pF[0:1, 1, :], OCF[:, 0:1], sqst[jt][:, :], oc == 0, oc == 7,
                 reads=[("ones_col_f",), (P, "sq", jt)], writes=[(P, "pF", 1)])
        k.ts("dve", mrow[0:1, :], pF[0:1, 0, :], 1.0 / D, None, ALU.mult,
             reads=[(P, "pF", 0)], writes=[(P, "mrow")])
        k.tt("dve", vrow[0:1, :], mrow[0:1, :], mrow[0:1, :], ALU.mult,
             reads=[(P, "mrow")], writes=[(P, "vrow")])
        k.stt("dve", vrow[0:1, :], pF[0:1, 1, :], 1.0 / D, vrow[0:1, :], ALU.mult, ALU.subtract,
              reads=[(P, "pF", 1), (P, "vrow")], writes=[(P, "vrow")])
        k.act(vrow[0:1, :], vrow[0:1, :], AF.Ln, bias=epsr[0:1, 0:1],
              reads=[(P, "vrow"), (P, "epsr")], writes=[(P, "vrow")])
        k.act(vrow[0:1, :], vrow[0:1, :], AF.Exp, scale=-0.5,
              reads=[(P, "vrow")], writes=[(P, "vrow")])
        k.mm(pF[:, 2, :], ORF[:, :], mrow[0:1, :], True, True,
             reads=[("ones_row_f",), (P, "mrow")], writes=[(P, "pF", 2)])
        k.mm(pF[:, 3, :], ORF[:, :], vrow[0:1, :], True, True,
             reads=[("ones_row_f",), (P, "vrow")], writes=[(P, "pF", 3)])
        for oc in range(8):
            jt = oc % 2
            k.tt("dve", zapp[:, :], zst[:, oc, :], pF[:, 2, :], ALU.subtract,
                 reads=[(P, "zst", oc), (P, "pF", 2)], writes=[(P, "zapp")])
            k.tt("dve", zapp[:, :], zapp[:, :], pF[:, 3, :], ALU.mult,
                 reads=[(P, "zapp"), (P, "pF", 3)], writes=[(P, "zapp")])
            k.ts("dve", zapp[:, :], zapp[:, :], lgc[:, oc:oc + 1], lbc[:, oc:oc + 1],
                 ALU.mult, ALU.add,
                 reads=[(P, "zapp"), (P, "lgc"), (P, "lbc")], writes=[(P, "zapp")])
            k.copy("dve", yhib[jt][:, :], zapp[:, :], reads=[(P, "zapp")], writes=[(P, "yh", jt)])
            k.copy("pool", yhif[jt][:, :], yhib[jt][:, :], reads=[(P, "yh", jt)],
                   writes=[(P, "yhf", jt)])
            k.tt("dve", ylob[jt][:, :], zapp[:, :], yhif[jt][:, :], ALU.subtract,
                 reads=[(P, "zapp"), (P, "yhf", jt)], writes=[(P, "yl", jt)])
            k.dma("sp", y2hi.ap()[128 * oc:128 * (oc + 1), ssl], yhib[jt][:, :], P + "y2o",
                  reads=[(P, "yh", jt)], writes=[(P, "y2h")])
            k.dma("sp", y2lo.ap()[128 * oc:128 * (oc + 1), ssl], ylob[jt][:, :], P + "y2o",
                  reads=[(P, "yl", jt)], writes=[(P, "y2l")])
    k.free(P + "_ylob0", P + "_ylob1", P + "_yhif0", P + "_yhif1", P + "_yhib0", P + "_yhib1",
           P + "_zapp", P + "_ytmp0", P + "_ytmp1", P + "_sqst0", P + "_sqst1", P + "_zst",
           P + "_wbc0", P + "_wbc1", P + "_wbc2", P + "_ystg0", P + "_ystg1", P + "_ystg2",
           P + "_cbuf0", P + "_cbuf1", P + "_cbuf2",
           P + "_vrow", P + "_mrow", P + "_epsr", P + "_lbc", P + "_lgc", P + "_biassb",
           P + "_W8s", P + "_w8stg0", P + "_w8stg1", P + "_wef", P + "_pos1", P + "_off8",
           P + "_offb", P + "_oc8", P + "_pm", P + "_ca", P + "_ls8", P + "_la8", P + "_ohb",
           P + "_cum", P + "_es8", P + "_gcol", P + "_g8", P + "_runbcs", P + "_runrow",
           P + "_locs", P + "_gvs", P + "_ohsf", P + "_lts", P + "_wgb", P + "_wgs",
           P + "_pF", P + "_y1fm")
    if upto == "y2":
        k.dma("sp", io["dbg"].ap(), y2hi.ap(), P + "dbg", reads=[(P, "y2h")], writes=[(P, "dbg")])
        return

    zhib = [k.sbuf(P + f"_zhib{i}", (128, D), BF16) for i in range(2)]
    zlob = [k.sbuf(P + f"_zlob{i}", (128, D), BF16) for i in range(2)]
    xrf = [k.sbuf(P + f"_xrf{i}", (128, D), F32) for i in range(2)]
    zf = k.sbuf(P + "_zf", (128, D), F32)
    ybt = k.sbuf(P + "_ybt", (128, D), BF16)
    for tt in range(32):
        tsl = slice(128 * tt, 128 * (tt + 1))
        j = tt % 2
        k.dma("sp", zhib[j][:, :], y2hi.ap()[:, tsl], P + "y2t",
              reads=[(P, "y2h")], writes=[(P, "zhi", j)], transpose=True)
        k.dma("sp", zlob[j][:, :], y2lo.ap()[:, tsl], P + "y2t",
              reads=[(P, "y2l")], writes=[(P, "zlo", j)], transpose=True)
        k.dma("sp", xrf[j][:, :], io["x_f32"].ap()[tsl, :], P + "xr2", writes=[(P, "xr2", j)])
        k.tt("dve", zf[:, :], zhib[j][:, :], zlob[j][:, :], ALU.add,
             reads=[(P, "zhi", j), (P, "zlo", j)], writes=[(P, "zf")])
        k.tt("dve", zf[:, :], zf[:, :], xrf[j][:, :], ALU.add,
             reads=[(P, "zf"), (P, "xr2", j)], writes=[(P, "zf")])
        lnk.apply(zf[:, :], zf[:, :], (P, "zf"), (P, "zf"))
        k.dma("sp", io["y_f32"].ap()[tsl, :], zf[:, :], P + "yo",
              reads=[(P, "zf")], writes=[(P, "yf")])
        k.copy("dve", ybt[:, :], zf[:, :], reads=[(P, "zf")], writes=[(P, "ybt")])
        k.dma("sp", io["y_bf16"].ap()[tsl, :], ybt[:, :], P + "yo",
              reads=[(P, "ybt")], writes=[(P, "yb")])
    k.free(P + "_ybt", P + "_zf", P + "_xrf0", P + "_xrf1",
           P + "_zlob0", P + "_zlob1", P + "_zhib0", P + "_zhib1")
    lnk.free()



_TC = None


def _tc(v):
    return np.ascontiguousarray(np.asarray(v).reshape(NCH, 128).T)


def _l0f_perm():
    order = []
    for p in range(22):
        lo, hi = 128 * p, min(128 * (p + 1), 2730)
        order += list(range(lo, hi))
        order += list(range(2730 + lo, 2730 + hi))
    return np.array(order)


def _decl(k, io, specs, pref=""):
    for nm, shape in specs:
        io[nm] = k.dram(pref + nm, shape, F32, kind="ExternalInput")


def _blk_l0a(k, io, inp, pref=""):
    _decl(k, io, pref=pref, specs=[("W_qk", (D, D)), ("W_v", (D, D)), ("slope", (1, 1))])
    base = {"W_qk": inp["L0_attn_W_qk"], "W_v": inp["L0_attn_W_v"],
            "slope": np.asarray(inp["L0_attn_alibi_slope"]).reshape(1, 1)}
    return [{pref + nm: v for nm, v in base.items()}] * 8


def _blk_l0f(k, io, inp, pref=""):
    specs = []
    for l in range(2):
        specs += [(f"wqkv{l}", (D, 1536)), (f"wo{l}", (512, D)),
                  (f"qs_col{l}", (128, 1)), (f"ks_col{l}", (128, 1)),
                  (f"w1_{l}", (D, 5460)), (f"w2_{l}", (2730, D)),
                  (f"b1_cols{l}", (128, 44)), (f"b2_row{l}", (1, D)),
                  (f"lg_cols{l}", (128, NCH)), (f"lb_cols{l}", (128, NCH))]
    _decl(k, io, specs, pref=pref)
    PERM = _l0f_perm()
    base = {}
    for l in range(2):
        b1p = np.asarray(inp["L0_ffn_b1"][l])[PERM]
        b1c = np.zeros((128, 44), np.float32)
        off = 0
        for oc in range(44):
            rows = 128 if oc < 42 else 42
            b1c[0:rows, oc] = b1p[off:off + rows]
            off += rows
        base[f"wqkv{l}"] = inp["L0_ffn_wqkv"][l]
        base[f"wo{l}"] = inp["L0_ffn_wo"][l]
        base[f"qs_col{l}"] = np.ascontiguousarray(
            np.tile(np.asarray(inp["L0_ffn_q_scale"][l]), 2).reshape(128, 1))
        base[f"ks_col{l}"] = np.ascontiguousarray(
            np.tile(np.asarray(inp["L0_ffn_k_scale"][l]), 2).reshape(128, 1))
        base[f"w1_{l}"] = np.ascontiguousarray(np.asarray(inp["L0_ffn_w1"][l])[:, PERM])
        base[f"w2_{l}"] = inp["L0_ffn_w2"][l]
        base[f"b1_cols{l}"] = b1c
        base[f"b2_row{l}"] = np.asarray(inp["L0_ffn_b2"][l]).reshape(1, D)
        base[f"lg_cols{l}"] = _tc(inp["L0_ffn_ln_g"][l])
        base[f"lb_cols{l}"] = _tc(inp["L0_ffn_ln_b"][l])
    return [{pref + nm: v for nm, v in base.items()}] * 8


def _blk_l1a(k, io, inp, pref=""):
    _decl(k, io, pref=pref, specs=[("Wq", (D, D)), ("Wk", (D, D)), ("Wv", (D, D)), ("Wout", (D, D)),
                  ("norm_w_cols", (128, NCH)), ("Wg32", (32, D)), ("bg32", (32, 1)),
                  ("bout_row", (1, D))])
    wg32 = np.zeros((32, D), np.float32)
    wg32[0:8] = inp["L1_attn_Wbeta"]
    wg32[8:16] = inp["L1_attn_Wig"]
    wg32[16:24] = inp["L1_attn_Wog"]
    bg32 = np.zeros((32, 1), np.float32)
    bg32[0:8, 0] = inp["L1_attn_bbeta"]
    bg32[8:16, 0] = inp["L1_attn_big"]
    bg32[16:24, 0] = inp["L1_attn_bog"]
    base = {"Wq": inp["L1_attn_Wq"], "Wk": inp["L1_attn_Wk"], "Wv": inp["L1_attn_Wv"],
            "Wout": inp["L1_attn_Wout"], "norm_w_cols": _tc(inp["L1_attn_norm_w"]),
            "Wg32": wg32, "bg32": bg32,
            "bout_row": np.asarray(inp["L1_attn_bout"]).reshape(1, D)}
    return [{pref + nm: v for nm, v in base.items()}] * 8


def _blk_l1f(k, io, inp, pref=""):
    _decl(k, io, pref=pref, specs=[("in_proj_w", (3072, D)), ("in_proj_b", (1, 3072)),
                  ("out_proj_w", (D, D)), ("out_proj_b", (1, D)),
                  ("ln1_g", (1, D)), ("ln1_b", (1, D)),
                  ("ln2_g_col", (128, 8)), ("ln2_b_col", (128, 8)), ("wg", (D, 8)),
                  ("my_w1", (D, 4096)), ("my_w2", (4096, D)),
                  ("my_b1_cols", (128, 32)), ("my_b2_row", (1, D)), ("pmask", (8, 1))])
    base = {"in_proj_w": inp["L1_ffn_in_proj_w"],
            "in_proj_b": np.asarray(inp["L1_ffn_in_proj_b"]).reshape(1, -1),
            "out_proj_w": inp["L1_ffn_out_proj_w"],
            "out_proj_b": np.asarray(inp["L1_ffn_out_proj_b"]).reshape(1, -1),
            "ln1_g": np.asarray(inp["L1_ffn_ln1_g"]).reshape(1, -1),
            "ln1_b": np.asarray(inp["L1_ffn_ln1_b"]).reshape(1, -1),
            "ln2_g_col": _tc(inp["L1_ffn_ln2_g"]), "ln2_b_col": _tc(inp["L1_ffn_ln2_b"]),
            "wg": inp["L1_ffn_wg"]}
    out = []
    for c in range(8):
        out.append(dict(base,
                        my_w1=np.ascontiguousarray(inp["L1_ffn_w1"][c]),
                        my_w2=np.ascontiguousarray(inp["L1_ffn_w2"][c]),
                        my_b1_cols=np.ascontiguousarray(
                            np.asarray(inp["L1_ffn_b1"][c]).reshape(32, 128).T),
                        my_b2_row=np.asarray(inp["L1_ffn_b2"][c]).reshape(1, -1),
                        pmask=(np.arange(8) < c).astype(np.float32).reshape(8, 1)))
    return [{pref + nm: v for nm, v in m.items()} for m in out]


def _blk_l2a(k, io, inp, pref=""):
    _decl(k, io, pref=pref, specs=[("Wq", (D, D)), ("Wk", (D, D)), ("Wv", (D, D)), ("Wo", (D, D))])
    base = {"Wq": inp["L2_attn_Wq"], "Wk": inp["L2_attn_Wk"],
            "Wv": inp["L2_attn_Wv"], "Wo": inp["L2_attn_Wo"]}
    return [{pref + nm: v for nm, v in base.items()}] * 8


def _blk_l2f(k, io, inp, pref=""):
    _decl(k, io, pref=pref, specs=[("W_r", (D, D)), ("W_k", (D, D)), ("W_v", (D, D)),
                  ("mu_r", (128, NCH)), ("mu_k", (128, NCH))])
    base = {"W_r": inp["L2_ffn_W_r"], "W_k": inp["L2_ffn_W_k"], "W_v": inp["L2_ffn_W_v"],
            "mu_r": _tc(inp["L2_ffn_mu_r"]), "mu_k": _tc(inp["L2_ffn_mu_k"])}
    return [{pref + nm: v for nm, v in base.items()}] * 8


def _blk_l3a(k, io, inp, pref=""):
    _decl(k, io, pref=pref, specs=[("Wq", (D, D)), ("Wk", (D, D)), ("Wv", (D, D)), ("Wo", (D, D)),
                  ("bq_col", (128, NCH)), ("bk_row", (1, D)), ("bv_row", (1, D)),
                  ("bo_row", (1, D))])
    base = {"Wq": inp["L3_attn_Wq"], "Wk": inp["L3_attn_Wk"], "Wv": inp["L3_attn_Wv"],
            "Wo": inp["L3_attn_Wo"], "bq_col": _tc(inp["L3_attn_bq"]),
            "bk_row": np.asarray(inp["L3_attn_bk"]).reshape(1, -1),
            "bv_row": np.asarray(inp["L3_attn_bv"]).reshape(1, -1),
            "bo_row": np.asarray(inp["L3_attn_bo"]).reshape(1, -1)}
    return [{pref + nm: v for nm, v in base.items()}] * 8


def _blk_l3f(k, io, inp, pref=""):
    _decl(k, io, pref=pref, specs=[("W1", (4, D, 4096)), ("W2", (4, 4096, D)), ("b1_cols", (4, 128, 32)),
                  ("b2_rows", (4, 1, D)), ("Wu_cols", (128, NCH)), ("bu", (1, 1))])
    base = {"W1": inp["L3_ffn_W1"], "W2": inp["L3_ffn_W2"],
            "b1_cols": np.ascontiguousarray(
                np.asarray(inp["L3_ffn_b1"]).reshape(4, 32, 128).transpose(0, 2, 1)),
            "b2_rows": np.asarray(inp["L3_ffn_b2"]).reshape(4, 1, D),
            "Wu_cols": _tc(inp["L3_ffn_Wu"]),
            "bu": np.asarray(inp["L3_ffn_bu"]).reshape(1, 1)}
    return [{pref + nm: v for nm, v in base.items()}] * 8


BLOCKS = [
    (host_consts_l0_attn, emit_l0_attn, _blk_l0a),
    (host_consts_l0_ffn, emit_l0_ffn, _blk_l0f),
    (host_consts_l1_attn, emit_l1_attn, _blk_l1a),
    (host_consts_l1_ffn, emit_l1_ffn, _blk_l1f),
    ((lambda k: (host_consts_l0_attn(k), host_consts_l2_attn(k))), emit_l2_attn, _blk_l2a),
    (None, emit_l2_ffn, _blk_l2f),
    (None, emit_l3_attn, _blk_l3a),
    (host_consts_l3_ffn, emit_l3_ffn, _blk_l3f),
]


def _kernel_unfused(**inputs):
    from concourse.bass_utils import run_bass_kernel_spmd
    x = np.ascontiguousarray(np.asarray(inputs["L0_attn_x"], np.float32))
    for hc, emit, blk in BLOCKS:
        k = K()
        emit_common_consts(k)
        if hc is not None:
            hc(k)
        x_d = k.dram("x", (S, D), F32, kind="ExternalInput")
        xb_d = k.dram("x_bf16", (S, D), BF16)
        y_d = k.dram("y", (S, D), F32, kind="ExternalOutput")
        yb_d = k.dram("y_bf16", (S, D), BF16)
        io = {"x_f32": x_d, "x_bf16": xb_d, "y_f32": y_d, "y_bf16": yb_d}
        extras = blk(k, io, inputs)
        k.dma("pool", xb_d.ap(), x_d.ap(), "prolog", writes=[("xbf",)])
        k.barrier()
        emit(k, io)
        nc = k.finalize()
        per_core = [dict(extras[c], x=np.ascontiguousarray(x[c])) for c in range(8)]
        res = run_bass_kernel_spmd(nc, k.in_maps(per_core), core_ids=list(range(8)))
        k.close()
        x = np.stack([np.asarray(res.results[c]["y"], np.float32) for c in range(8)])
    return x


def kernel(**inputs):
    from concourse.bass_utils import run_bass_kernel_spmd
    x = np.ascontiguousarray(np.asarray(inputs["L0_attn_x"], np.float32))
    k = K()
    emit_common_consts(k)
    for hc in (host_consts_l0_attn, host_consts_l0_ffn, host_consts_l1_attn,
               host_consts_l1_ffn, host_consts_l2_attn, host_consts_l3_ffn):
        hc(k)
    x_d = k.dram("x", (S, D), F32, kind="ExternalInput")
    y_d = k.dram("y", (S, D), F32, kind="ExternalOutput")
    nb = len(BLOCKS)
    f32s = [x_d] + [k.dram(f"hx{i}_f32", (S, D), F32) for i in range(1, nb)] + [y_d]
    bf16s = [k.dram(f"hx{i}_bf16", (S, D), BF16) for i in range(nb + 1)]
    k.dma("pool", bf16s[0].ap(), x_d.ap(), "prolog", writes=[("xbf",)])
    k.barrier()
    maps = [dict() for _ in range(8)]
    prefs = ["b0_", "b1_", "b2_", "b3_", "b4_", "b5_", "b6_", "b7_"]
    for bi, (hc, emit, blk) in enumerate(BLOCKS):
        io = {"x_f32": f32s[bi], "x_bf16": bf16s[bi],
              "y_f32": f32s[bi + 1], "y_bf16": bf16s[bi + 1]}
        extras = blk(k, io, inputs, pref=prefs[bi])
        for c in range(8):
            maps[c].update(extras[c])
        emit(k, io)
        k.barrier()
    nc = k.finalize()
    for c in range(8):
        maps[c]["x"] = np.ascontiguousarray(x[c])
    res = run_bass_kernel_spmd(nc, k.in_maps(maps), core_ids=list(range(8)))
    k.close()
    return np.stack([np.asarray(res.results[c]["y"], np.float32) for c in range(8)])
